# version 20
# baseline (speedup 1.0000x reference)
import contextlib

import numpy as np

import concourse.bass as bass
import concourse.mybir as mybir
from concourse.bass_utils import run_bass_kernel_spmd

F32 = mybir.dt.float32
BF16 = mybir.dt.bfloat16
AF = mybir.ActivationFunctionType
ALU = mybir.AluOpType
AX = mybir.AxisListType.X

B, L, D = 4, 2048, 2048
INTER = 2048
CSTAB = 16.0
NEGBIG = -1.0e30
EPS = 1e-5

LOC_NKT = (8, 12)
LOC_DS = (4, 8)
GLB_NKT = (12, 16)
GLB_DS = (8, 12)

ENGS = ("gpsimd", "sync", "tensor", "vector", "scalar")
COMPUTE_SEMS = ("te", "ve", "se")
SEMNAMES = ("te", "ve", "se", "ge", "lc", "st1", "lw2a", "lw2b", "lxa0", "lxa1", "lo0", "lo1",
            "lwq", "lwq1", "lwk", "lwk1", "lwv", "lwv1", "lwo", "lwo1", "lbo", "lw1", "lw11", "lw3", "lw31", "lwr",
            "ltap", "st")


class Sched:

    def __init__(self):
        self.ops = []
        self.counts = {}
        self.val = {}
        self.emitted = 0

    def op(self, eng, fn, inc=None, after=(), n=1):
        ev = len(self.ops)
        if inc is not None:
            amt = (1 if inc in COMPUTE_SEMS else 16) * n
            self.counts[inc] = self.counts.get(inc, 0) + amt
            self.val[ev] = (inc, self.counts[inc])
        flat = []
        for a in after:
            if a is None:
                continue
            if isinstance(a, (list, tuple)):
                flat.extend(x for x in a if x is not None)
            else:
                flat.append(a)
        self.ops.append((eng, fn, inc, flat))
        return ev

    def emit(self, blk, sems, fence=None):
        lo, hi = self.emitted, len(self.ops)
        self.emitted = hi
        for name in ENGS:
            mine = [i for i in range(lo, hi) if self.ops[i][0] == name]
            if not mine:
                continue

            def body(e, name=name, mine=mine):
                if fence:
                    for s, v in fence.items():
                        if v > 0:
                            e.wait_ge(sems[s], v)
                for i in mine:
                    _, fn, inc, after = self.ops[i]
                    need = {}
                    for ev in after:
                        assert ev in self.val, f"event {ev} has no sem inc"
                        s, v = self.val[ev]
                        if need.get(s, 0) < v:
                            need[s] = v
                    for s, v in sorted(need.items()):
                        e.wait_ge(sems[s], v)
                    ins = fn(e)
                    if inc is not None:
                        amt = 1 if inc in COMPUTE_SEMS else 16
                        if isinstance(ins, (list, tuple)):
                            for one in ins:
                                one.then_inc(sems[inc], amt)
                        else:
                            ins.then_inc(sems[inc], amt)

            getattr(blk, name)(body)
        return dict(self.counts)


def build_main(debug_taps=()):
    nc = bass.Bass("TRN2", target_bir_lowering=False)
    ap = lambda *a, **kw: nc.dram_tensor(*a, **kw).ap()

    x_in = ap("x_in", [L, D], F32, kind="ExternalInput")
    wq_img = ap("wq_img", [16, 128, 2048], F32, kind="ExternalInput")
    wk_img = ap("wk_img", [16, 128, 2048], F32, kind="ExternalInput")
    wvl_img = ap("wvl_img", [5, 128, 16, 384], F32, kind="ExternalInput")
    wvg_img = ap("wvg_img", [128, 16, 128], F32, kind="ExternalInput")
    wo_img = ap("wo_img", [4, 128, 16, 512], F32, kind="ExternalInput")
    bo_in = ap("bo_in", [1, 2048], F32, kind="ExternalInput")
    w1_img = ap("w1_img", [16, 128, 2048], F32, kind="ExternalInput")
    w3_img = ap("w3_img", [4, 128, 16, 512], F32, kind="ExternalInput")
    wr_img = ap("wr_img", [128, 16, 8], F32, kind="ExternalInput")
    cos_in = ap("cos_in", [128, 2048], F32, kind="ExternalInput")
    sin_in = ap("sin_in", [128, 2048], F32, kind="ExternalInput")
    biasv_in = ap("biasv_in", [128, 56], F32, kind="ExternalInput")
    tri_in = ap("tri_in", [128, 128], F32, kind="ExternalInput")
    idn_in = ap("idn_in", [128, 128], F32, kind="ExternalInput")
    sel2_in = ap("sel2_in", [2, 128], F32, kind="ExternalInput")
    pswap_in = ap("pswap_in", [128, 128], F32, kind="ExternalInput")

    w2_img = ap("w2_img", [16, 128, 2048], F32, kind="ExternalInput")
    out_x2 = ap("out_x2", [1024, D], F32, kind="ExternalOutput")
    out_g = ap("out_g", [128, 128], F32, kind="ExternalOutput")
    out_ps = ap("out_ps", [8, 2048], F32, kind="ExternalOutput")
    taps = {}
    tapshapes = {"t_x1f": [128, 32768], "t_y": [128, 16384],
                 "t_h1": [128, 16384], "t_wg": [128, 64],
                 "t_x3f": [128, 16384]}
    for name in debug_taps:
        taps[name] = ap(name, tapshapes[name], F32, kind="ExternalOutput")

    S = Sched()

    with contextlib.ExitStack() as ctx:
        E = ctx.enter_context
        sems = {k: E(nc.semaphore(k)) for k in SEMNAMES}

        biasv = E(nc.sbuf_tensor("biasv", [128, 56], F32))
        tri = E(nc.sbuf_tensor("tri", [128, 128], BF16))
        idn = E(nc.sbuf_tensor("idn", [128, 128], BF16))
        sel2 = E(nc.sbuf_tensor("sel2", [2, 128], BF16))
        pswap = E(nc.sbuf_tensor("pswap", [128, 128], BF16))
        stats = E(nc.sbuf_tensor("stats", [128, 16], F32))
        rab216 = E(nc.sbuf_tensor("rab216", [1, 2048], BF16))

        ps0 = E(nc.psum_tensor("ps0", [128, 1024], F32))
        ps1 = E(nc.psum_tensor("ps1", [128, 1024], F32))
        ps2 = E(nc.psum_tensor("ps2", [128, 512], F32))
        ps3 = E(nc.psum_tensor("ps3", [128, 512], F32))
        ps4 = E(nc.psum_tensor("ps4", [128, 512], F32))

        psu_last = {}

        def ln_stats(src_ap, pre, xsq_ap):
            e_sum = S.op("vector", lambda v: v.tensor_reduce(
                stats[:, 0:1], src_ap, AX, ALU.add), inc="ve", after=pre)
            e_sq = S.op("scalar", lambda s: s.activation(
                xsq_ap, src_ap, AF.Square, accum_out=stats[:, 1:2]), inc="se",
                after=pre)

            c1 = S.op("vector", lambda v: v.tensor_scalar(
                stats[:, 2:3], stats[:, 0:1], 1.0 / D, None, ALU.mult),
                inc="ve", after=[e_sum])
            c2 = S.op("vector", lambda v: v.tensor_tensor(
                stats[:, 4:5], stats[:, 2:3], stats[:, 2:3], ALU.mult),
                inc="ve", after=[c1])
            c3 = S.op("vector", lambda v: v.tensor_scalar(
                stats[:, 3:4], stats[:, 1:2], 1.0 / D, None, ALU.mult),
                inc="ve", after=[e_sq])
            c4 = S.op("vector", lambda v: v.tensor_tensor(
                stats[:, 3:4], stats[:, 3:4], stats[:, 4:5], ALU.subtract),
                inc="ve", after=[c2, c3])
            c5 = S.op("vector", lambda v: v.tensor_scalar(
                stats[:, 3:4], stats[:, 3:4], EPS, None, ALU.add),
                inc="ve", after=[c4])
            e_ch = S.op("vector", lambda v: v.reciprocal(
                stats[:, 5:6], stats[:, 3:4]), inc="ve", after=[c5])
            e_rstd = S.op("scalar", lambda s: s.activation(
                stats[:, 6:7], stats[:, 5:6], AF.Sqrt), inc="se", after=[e_ch])
            return [c1, e_ch, e_rstd]

        pair_ymuls = {}
        d_done = []
        with contextlib.ExitStack() as ctxY:
            ys_stack = contextlib.ExitStack()
            gsb = ctxY.enter_context(nc.sbuf_tensor("gsb", [128, 128], F32))
            with contextlib.ExitStack() as ctxA:
                EA = ctxA.enter_context
                x1f = EA(nc.sbuf_tensor("x1f", [128, 16 * 2048], BF16))
                v65l = EA(nc.sbuf_tensor("v65l", [128, 12 * 1950], BF16))
                v65g = EA(nc.sbuf_tensor("v65g", [128, 16 * 130], BF16))
                cosT = EA(nc.sbuf_tensor("cosT", [128, 2048], BF16))
                sinT = EA(nc.sbuf_tensor("sinT", [128, 2048], BF16))
                def x1f_c(dc, t0, n):
                    return x1f[:, dc * 2048 + t0: dc * 2048 + t0 + n]

                with contextlib.ExitStack() as ctxB:
                    EB = ctxB.enter_context
                    stage = EB(nc.sbuf_tensor("stage", [128, 2 * 2048], F32))
                    xsq = EB(nc.sbuf_tensor("xsq", [128, 2048], F32))
                    x1tok = EB(nc.sbuf_tensor("x1tok", [128, 2 * 2048], BF16))
                    wvbuf = EB(nc.sbuf_tensor("wvbuf", [128, 2 * 6144], BF16))
                    pst = EB(nc.psum_tensor("pst", [128, 1024], BF16))
                    blk = EB(nc.Block())
                    for src, dst in ((cos_in, cosT), (sin_in, sinT),
                                     (biasv_in, biasv), (tri_in, tri),
                                     (idn_in, idn), (sel2_in, sel2),
                                     (pswap_in, pswap)):
                        e_c = S.op("gpsimd", lambda g, s=src, d=dst:
                                   g.dma_start(out=d[:], in_=s), inc="lc")
                    e_consts = e_c
                    e_ones_l = S.op("vector", lambda v: v.memset(
                        v65l[:].rearrange("p (a o) -> p a o", o=65)[:, :, 64:65],
                        1.0), inc="ve")
                    e_ones_g = S.op("vector", lambda v: v.memset(
                        v65g[:].rearrange("p (a o) -> p a o", o=65)[:, :, 64:65],
                        1.0), inc="ve")
                    wv_users = {0: [], 1: []}
                    e_wv_d = {}

                    def emit_wv_load(vc):
                        vp = vc % 2
                        e_wv_d[vc] = S.op("gpsimd", lambda g, vc=vc, vp=vp:
                                          g.dma_start(
                            out=wvbuf[:, vp * 6144:(vp + 1) * 6144].rearrange(
                                "p (dc n) -> p dc n", dc=16),
                            in_=wvl_img[vc]), inc=("lwv", "lwv1")[vp],
                            after=list(wv_users[vp]))
                        wv_users[vp] = []
                    emit_wv_load(0)
                    emit_wv_load(1)

                    vl_pend = []
                    vstate = {"n": 0, "last": None}

                    def emit_vl(vc, tt, ready, defer=False):
                        vp = vc % 2
                        pid = vstate["n"] % 3
                        vstate["n"] += 1
                        psu = (ps2, ps3, ps4)[pid]
                        def mm(t, vc=vc, tt=tt, psu=psu, vp=vp):
                            for dc in range(16):
                                ins = t.matmul(
                                    psu[:, 0:384],
                                    x1f_c(dc, 512 + tt * 128, 128),
                                    wvbuf[:, vp * 6144 + dc * 384:
                                          vp * 6144 + (dc + 1) * 384],
                                    start=(dc == 0), stop=(dc == 15))
                            return ins
                        e_mm = S.op("tensor", mm, inc="te",
                                    after=[e_wv_d[vc], ready,
                                           psu_last.get(pid)])
                        wv_users[vp].append(e_mm)

                        def emit_ev(vc=vc, tt=tt, psu=psu, pid=pid, e_mm=e_mm):
                            def evac(v, vc=vc, tt=tt, psu=psu):
                                src = psu[:, 0:384].rearrange(
                                    "p (h d) -> p h d", d=64)
                                dst = v65l[:, tt * 1950 + vc * 390:
                                           tt * 1950 + vc * 390 + 390].rearrange(
                                    "p (h o) -> p h o", o=65)[:, :, 0:64]
                                return v.tensor_copy(dst, src)
                            ev = S.op("vector", evac, inc="ve", after=[e_mm])
                            psu_last[pid] = ev
                            vstate["last"] = ev
                        if defer:
                            vl_pend.append(emit_ev)
                        else:
                            emit_ev()

                    e_apply = {}
                    e_evac = {}
                    e_tr_last = {}
                    for tt in range(16):
                        par = tt % 2
                        st_ap = stage[:, par * 2048:(par + 1) * 2048]
                        xt_ap = x1tok[:, par * 2048:(par + 1) * 2048]
                        e_load = S.op("sync", lambda sy, tt=tt, st_ap=st_ap:
                                      sy.dma_start(out=st_ap,
                                                   in_=x_in[tt * 128:(tt + 1) * 128, :]),
                                      inc=("lxa0", "lxa1")[par],
                                      after=[e_apply.get(tt - 2)])
                        pre = ln_stats(st_ap, [e_load], xsq[:])
                        e_apply[tt] = S.op("vector", lambda v, st_ap=st_ap, xt_ap=xt_ap:
                                           v.tensor_scalar(xt_ap, st_ap,
                                                           stats[:, 2:3], stats[:, 6:7],
                                                           ALU.subtract, ALU.mult),
                                           inc="ve",
                                           after=pre + [e_tr_last.get(tt - 2)])
                        for fn in vl_pend:
                            fn()
                        del vl_pend[:]
                        for h in range(2):
                            def trs(t, xt_ap=xt_ap, h=h):
                                out = []
                                for dci in range(8):
                                    dc = h * 8 + dci
                                    out.append(t.transpose(
                                        pst[:, dci * 128:(dci + 1) * 128],
                                        xt_ap[:, dc * 128:(dc + 1) * 128], idn[:]))
                                return out
                            e_tr = S.op("tensor", trs, inc="te", n=8,
                                        after=[e_apply[tt], e_consts,
                                               e_evac.get((tt, h - 1) if h else (tt - 1, 1))])
                            def evac(s, tt=tt, h=h):
                                src = pst[:, 0:1024].rearrange(
                                    "p (dc t) -> p dc t", dc=8)
                                dst = x1f[:].rearrange(
                                    "p (dc t) -> p dc t", dc=16)[
                                        :, h * 8:(h + 1) * 8, tt * 128:(tt + 1) * 128]
                                return s.copy(dst, src)
                            e_evac[(tt, h)] = S.op("scalar", evac, inc="se",
                                                   after=[e_tr])
                            e_tr_last[tt] = e_tr
                        if tt >= 5:
                            emit_vl(1, tt - 5, e_evac[(tt - 1, 1)], defer=True)
                        if tt >= 4:
                            emit_vl(0, tt - 4, e_evac[(tt, 1)], defer=True)
                    e_x1f = e_evac[(15, 1)]
                    for fn in vl_pend:
                        fn()
                    del vl_pend[:]
                    emit_vl(1, 11, e_x1f)
                    emit_wv_load(2)
                    emit_wv_load(3)
                    for vc in (2, 3):
                        for tt in range(12):
                            emit_vl(vc, tt, e_x1f)
                        if vc == 2:
                            emit_wv_load(4)
                    e_wvg = S.op("gpsimd", lambda g: g.dma_start(
                        out=wvbuf[:, 6144:6144 + 2048].rearrange(
                            "p (dc n) -> p dc n", dc=16),
                        in_=wvg_img), inc="lwv1", after=list(wv_users[1]))
                    for tt in range(12):
                        emit_vl(4, tt, e_x1f)
                    for tt in range(16):
                        pid = vstate["n"] % 3
                        vstate["n"] += 1
                        psu = (ps2, ps3, ps4)[pid]
                        def mmg(t, tt=tt, psu=psu):
                            for dc in range(16):
                                ins = t.matmul(psu[:, 0:128],
                                               x1f_c(dc, tt * 128, 128),
                                               wvbuf[:, 6144 + dc * 128:
                                                     6144 + (dc + 1) * 128],
                                               start=(dc == 0), stop=(dc == 15))
                            return ins
                        e_mm = S.op("tensor", mmg, inc="te",
                                    after=[e_wvg, e_x1f, psu_last.get(pid)])
                        def evacg(v, tt=tt, psu=psu):
                            src = psu[:, 0:128].rearrange("p (h d) -> p h d", d=64)
                            dst = v65g[:, tt * 130:(tt + 1) * 130].rearrange(
                                "p (h o) -> p h o", o=65)[:, :, 0:64]
                            return v.tensor_copy(dst, src)
                        ev = S.op("vector", evacg, inc="ve", after=[e_mm])
                        psu_last[pid] = ev
                        vstate["last"] = ev
                    psu_last[0] = psu_last[1] = psu_last[2] = vstate["last"]
                    fence = S.emit(blk, sems)
                e_x1f_all = [e_evac[(tt, h)] for tt in range(16) for h in range(2)]
                y_sb = ys_stack.enter_context(
                    nc.sbuf_tensor("y_sb", [128, 16 * 1024], BF16, side="right"))

                e_taps = []
                if "t_x1f" in taps:
                    with nc.Block() as blk:
                        e_taps.append(S.op("gpsimd", lambda g: g.dma_start(
                            out=taps["t_x1f"], in_=x1f[:]), inc="ltap",
                            after=[e_x1f]))
                        S.emit(blk, sems)

                with contextlib.ExitStack() as ctxD:
                    ED = ctxD.enter_context
                    wqb = ED(nc.sbuf_tensor("wqb", [128, 2 * 2048], BF16))
                    wkb = ED(nc.sbuf_tensor("wkb", [128, 2 * 2048], BF16))
                    qraw = ED(nc.sbuf_tensor("qraw", [128, 1024], BF16))
                    kraw = ED(nc.sbuf_tensor("kraw", [128, 2048], BF16))
                    sinscr = ED(nc.sbuf_tensor("sinscr", [128, 2048], BF16))
                    qrot = ED(nc.sbuf_tensor("qrot", [128, 2 * 1024], BF16))
                    krot = ED(nc.sbuf_tensor("krot", [128, 2 * 2048], BF16))
                    psb = ED(nc.sbuf_tensor("psb", [128, 2048], BF16))
                    rbc = ED(nc.sbuf_tensor("rbc", [128, 2048], BF16))
                    onesd = ED(nc.sbuf_tensor("onesd", [1, 128], BF16))
                    ps5 = ED(nc.psum_tensor("ps5", [128, 512], F32))
                    blk = ED(nc.Block())
                    e_onesd = S.op("vector", lambda v: v.memset(
                        onesd[:], 1.0), inc="ve")
                    e_exp_par = {}
                    e_pv_lastk = {}
                    r45 = {0: psu_last.get(2), 1: None}
                    tail_state = {"pend": None, "slotg": {}, "bcg": {},
                                  "slot": 0, "ymuls": []}

                    def emit_tail():
                        st = tail_state["pend"]
                        if st is None:
                            return
                        tail_state["pend"] = None
                        sl, cols, e_rA, e_rB, e_yA, e_yB = st
                        e_bA = S.op("tensor", lambda t: t.matmul(
                            ps4[:], onesd[:],
                            rab216[0:1, sl * 1024:sl * 1024 + 512],
                            start=True, stop=True), inc="te",
                            after=[e_rA, r45[0], e_onesd])
                        e_bB = S.op("tensor", lambda t: t.matmul(
                            ps5[:], onesd[:],
                            rab216[0:1, sl * 1024 + 512:(sl + 1) * 1024],
                            start=True, stop=True), inc="te",
                            after=[e_rB, r45[1], e_onesd])
                        tail_state["bcg"][sl] = [e_bA, e_bB]
                        prevm = tail_state["slotg"].get(sl, [None, None])
                        e_eA = S.op("scalar", lambda s: s.copy(
                            rbc[:, sl * 1024:sl * 1024 + 512], ps4[:]),
                            inc="se", after=[e_bA, prevm[0]])
                        e_eB = S.op("scalar", lambda s: s.copy(
                            rbc[:, sl * 1024 + 512:(sl + 1) * 1024], ps5[:]),
                            inc="se", after=[e_bB, prevm[1]])
                        r45[0] = e_eA
                        r45[1] = e_eB
                        e_mA = S.op("vector", lambda v: v.tensor_tensor(
                            y_sb[0:64, cols[0]:cols[1]],
                            y_sb[0:64, cols[0]:cols[1]],
                            rbc[0:64, sl * 1024:sl * 1024 + 512], ALU.mult),
                            inc="ve", after=[e_eA, e_yA])
                        e_mB = S.op("vector", lambda v: v.tensor_tensor(
                            y_sb[64:128, cols[0]:cols[1]],
                            y_sb[64:128, cols[0]:cols[1]],
                            rbc[64:128, sl * 1024 + 512:(sl + 1) * 1024],
                            ALU.mult), inc="ve", after=[e_eB, e_yB])
                        tail_state["slotg"][sl] = [e_mA, e_mB]
                        tail_state["ymuls"] += [e_mA, e_mB]
                        pair_ymuls[cols[0] // 1024] = e_mB
                    wload_gate = {0: [], 1: []}
                    qk_last_pair = {}
                    e_prep = {}
                    e_attn_prev = None

                    def prep_pair(p):
                        is_g = (p == 15)
                        kcols = 2048 if is_g else 1536
                        kbase = 0 if is_g else 512
                        pp = p % 2
                        e_wq = S.op("gpsimd", lambda g, p=p, pp=pp: g.dma_start(
                            out=wqb[:, pp * 2048:(pp + 1) * 2048],
                            in_=wq_img[p]), inc=("lwq", "lwq1")[pp],
                            after=wload_gate[pp])
                        e_wk = S.op("gpsimd", lambda g, p=p, pp=pp: g.dma_start(
                            out=wkb[:, pp * 2048:(pp + 1) * 2048],
                            in_=wk_img[p]), inc=("lwk", "lwk1")[pp],
                            after=wload_gate[pp])
                        wload_gate[pp] = []
                        e_ev = {}
                        rot_par = [0]
                        def mm45(fn, pre):
                            pid = rot_par[0]
                            rot_par[0] ^= 1
                            psu = (ps4, ps5)[pid]
                            e = S.op("tensor", lambda t, fn=fn, psu=psu:
                                     fn(t, psu), inc="te",
                                     after=pre + [r45[pid]])
                            return e, pid, psu
                        for which in ("q", "k"):
                            buf, raw = (wqb, qraw) if which == "q" else (wkb, kraw)
                            base = 1024 if which == "q" else kbase
                            ntc = 2 if which == "q" else kcols // 512
                            e_w = e_wq if which == "q" else e_wk
                            for tc in range(ntc):
                                def projf(t, psu, buf=buf, tc=tc, base=base,
                                          pp=pp):
                                    for dc in range(16):
                                        ins = t.matmul(
                                            psu[:],
                                            buf[:, pp * 2048 + dc * 128:
                                                pp * 2048 + (dc + 1) * 128],
                                            x1f_c(dc, base + tc * 512, 512),
                                            start=(dc == 0), stop=(dc == 15))
                                    return ins
                                e_mm, pid, psu = mm45(projf, [e_w, e_x1f])
                                wload_gate[pp].append(e_mm)
                                e_e = S.op("vector",
                                           lambda v, raw=raw, tc=tc, psu=psu:
                                           v.tensor_copy(
                                               raw[:, tc * 512:(tc + 1) * 512],
                                               psu[:]),
                                           inc="ve", after=[e_mm])
                                r45[pid] = e_e
                                e_ev[(which, tc)] = e_e

                        def rope(which, raw, rotb, rot0, ncols, tab0):
                            e_last = None
                            for tc in range(ncols // 512):
                                t0 = tc * 512
                                def swf(t, psu, raw=raw, t0=t0):
                                    return t.matmul(psu[:, 0:512], pswap[:],
                                                    raw[:, t0:t0 + 512],
                                                    start=True, stop=True)
                                e_sw, pid, psu = mm45(
                                    swf, [e_ev[(which, tc)], e_consts,
                                          qk_last_pair.get(pp)])
                                e_cm = S.op("vector", lambda v, raw=raw, t0=t0,
                                            tab0=tab0, rot0=rot0:
                                            v.tensor_tensor(
                                    rotb[:, rot0 + t0:rot0 + t0 + 512],
                                    raw[:, t0:t0 + 512],
                                    cosT[:, tab0 + t0:tab0 + t0 + 512],
                                    ALU.mult), inc="ve",
                                    after=[e_ev[(which, tc)], e_consts,
                                           qk_last_pair.get(pp)])
                                e_sm = S.op("vector", lambda v, t0=t0,
                                            tab0=tab0, psu=psu:
                                            v.tensor_tensor(
                                    sinscr[:, t0:t0 + 512], psu[:, 0:512],
                                    sinT[:, tab0 + t0:tab0 + t0 + 512],
                                    ALU.mult), inc="ve", after=[e_sw])
                                r45[pid] = e_sm
                                e_last = S.op("vector", lambda v, t0=t0,
                                              rot0=rot0: v.tensor_tensor(
                                    rotb[:, rot0 + t0:rot0 + t0 + 512],
                                    rotb[:, rot0 + t0:rot0 + t0 + 512],
                                    sinscr[:, t0:t0 + 512], ALU.add),
                                    inc="ve", after=[e_cm, e_sm])
                            return e_last
                        e_rq = rope("q", qraw, qrot, pp * 1024, 1024, 1024)
                        e_rk = rope("k", kraw, krot, pp * 2048, kcols, kbase)
                        e_prep[p] = (e_rq, e_rk)

                    prep_pair(0)
                    for p in range(16):
                        if p + 1 < 16:
                            prep_pair(p + 1)
                        is_g = (p == 15)
                        nktl, dsl = (GLB_NKT, GLB_DS) if is_g else (LOC_NKT, LOC_DS)
                        vt, vstride = (v65g, 130) if is_g else (v65l, 1950)
                        pp = p % 2
                        e_rq, e_rk = e_prep.pop(p)
                        e_qk_events = []

                        for qc in range(2):
                            nkt, ds = nktl[qc], dsl[qc]
                            e_pv = {}
                            pend = None
                            for kt in range(nkt + 1):
                                if kt == 2:
                                    emit_tail()
                                if kt < nkt:
                                    j0 = max(0, kt - ds)
                                    qo = j0 * 128
                                    par = kt % 2
                                    psu = (ps0, ps1)[par]
                                    slot = (24 + qc * 16 + kt) if is_g else (qc * 12 + kt)

                                    def qk(t, kt=kt, qc=qc, qo=qo, psu=psu,
                                           pp=pp):
                                        a = t.matmul(
                                            psu[0:128, qo:512],
                                            krot[0:64, pp * 2048 + kt * 128:
                                                 pp * 2048 + (kt + 1) * 128],
                                            qrot[0:64, pp * 1024 + qc * 512 + qo:
                                                 pp * 1024 + (qc + 1) * 512],
                                            start=True, stop=True)
                                        b = t.matmul(
                                            psu[0:128, 512 + qo:1024],
                                            krot[64:128, pp * 2048 + kt * 128:
                                                 pp * 2048 + (kt + 1) * 128],
                                            qrot[64:128, pp * 1024 + qc * 512 + qo:
                                                 pp * 1024 + (qc + 1) * 512],
                                            start=True, stop=True)
                                        return [a, b]
                                    e_qk = S.op("tensor", qk, inc="te", n=2,
                                                after=[e_rq, e_rk,
                                                       e_exp_par.get(par)])
                                    e_qk_events.append(e_qk)
                                if pend is not None:
                                    pkt, pgate = pend
                                    pqo = max(0, pkt - ds) * 128
                                    ppar = pkt % 2
                                    for hh in range(2):
                                        pvp = (ps2, ps3)[hh]
                                        vcol0 = (hh if is_g else 2 * p + hh) * 65
                                        def pv(t, hh=hh, pkt=pkt, pqo=pqo,
                                               ppar=ppar, pvp=pvp, vt=vt,
                                               vstride=vstride, vcol0=vcol0,
                                               nkt=nkt):
                                            return t.matmul(
                                                pvp[0:65, pqo:512],
                                                vt[:, pkt * vstride + vcol0:
                                                   pkt * vstride + vcol0 + 65],
                                                psb[:, ppar * 1024 + hh * 512 + pqo:
                                                    ppar * 1024 + hh * 512 + 512],
                                                start=(pkt == 0),
                                                stop=(pkt == nkt - 1))
                                        pre_pv = [pgate]
                                        if pkt == 0:
                                            pre_pv += [psu_last.get(hh),
                                                       e_ones_l, e_ones_g]
                                        e_pv[(pkt, hh)] = S.op(
                                            "tensor", pv, inc="te",
                                            after=pre_pv)
                                    e_pv_lastk[ppar] = e_pv[(pkt, 1)]
                                    pend = None
                                if kt < nkt:
                                    def expf(s, par=par, qo=qo, psu=psu,
                                             slot=slot):
                                        a = s.activation(
                                            psb[:, par * 1024 + qo:
                                                par * 1024 + 512],
                                            psu[0:128, qo:512], AF.Exp,
                                            bias=biasv[:, slot:slot + 1],
                                            scale=0.125)
                                        b = s.activation(
                                            psb[:, par * 1024 + 512 + qo:
                                                par * 1024 + 1024],
                                            psu[0:128, 512 + qo:1024], AF.Exp,
                                            bias=biasv[:, slot:slot + 1],
                                            scale=0.125)
                                        return [a, b]
                                    e_ex = S.op("scalar", expf, inc="se", n=2,
                                                after=[e_qk,
                                                       e_pv_lastk.get(par),
                                                       e_consts])
                                    e_exp_par[par] = e_ex
                                    e_gate = e_ex
                                    if kt >= ds:
                                        def trif(v, par=par, qo=qo):
                                            a = v.tensor_tensor(
                                                psb[:, par * 1024 + qo:
                                                    par * 1024 + qo + 128],
                                                psb[:, par * 1024 + qo:
                                                    par * 1024 + qo + 128],
                                                tri[:], ALU.mult)
                                            b = v.tensor_tensor(
                                                psb[:, par * 1024 + 512 + qo:
                                                    par * 1024 + 512 + qo + 128],
                                                psb[:, par * 1024 + 512 + qo:
                                                    par * 1024 + 512 + qo + 128],
                                                tri[:], ALU.mult)
                                            return [a, b]
                                        e_gate = S.op("vector", trif, inc="ve",
                                                      n=2, after=[e_ex])
                                    pend = (kt, e_gate)

                            sl = tail_state["slot"]
                            tail_state["slot"] ^= 1
                            cols = (p * 1024 + qc * 512,
                                    p * 1024 + (qc + 1) * 512)
                            pg = tail_state["slotg"].get(sl, [None, None])
                            bcg = tail_state["bcg"].get(sl, [None, None])
                            e_yA = S.op("scalar", lambda s, cols=cols:
                                        s.copy(y_sb[0:64, cols[0]:cols[1]],
                                               ps2[0:64, :]), inc="se",
                                        after=[e_pv[(nkt - 1, 0)], pg[0]])
                            e_yB = S.op("scalar", lambda s, cols=cols:
                                        s.copy(y_sb[64:128, cols[0]:cols[1]],
                                               ps3[0:64, :]), inc="se",
                                        after=[e_pv[(nkt - 1, 1)], pg[1]])
                            def _recip(v, dst0, src):
                                with nc.allow_low_precision(
                                        reason="softmax 1/denom as bf16; "
                                        "bf16-matmul-grade factor"):
                                    return v.reciprocal(
                                        rab216[0:1, dst0:dst0 + 512], src)
                            e_rA = S.op("vector", lambda v, sl=sl:
                                        _recip(v, sl * 1024, ps2[64:65, :]),
                                        inc="ve",
                                        after=[e_pv[(nkt - 1, 0)], bcg[0]])
                            e_rB = S.op("vector", lambda v, sl=sl:
                                        _recip(v, sl * 1024 + 512,
                                               ps3[64:65, :]), inc="ve",
                                        after=[e_pv[(nkt - 1, 1)], bcg[1]])
                            psu_last[0] = [e_yA, e_rA]
                            psu_last[1] = [e_yB, e_rB]
                            tail_state["pend"] = (sl, cols, e_rA, e_rB,
                                                  e_yA, e_yB)
                        qk_last_pair[pp] = e_qk_events[-1]
                    emit_tail()
                    e_y_done = tail_state["ymuls"][-1]
                    psu_last[0] = tail_state["ymuls"][-2]
                    psu_last[1] = tail_state["ymuls"][-1]
                    psu_last[2] = r45[0]
                    for _pid in (3, 4, 5, 6):
                        psu_last[_pid] = [e_exp_par[0], e_exp_par[1], r45[1]]
                    fence = S.emit(blk, sems)

            if "t_y" in taps:
                with nc.Block() as blk:
                    e_taps.append(S.op("gpsimd", lambda g: g.dma_start(
                        out=taps["t_y"], in_=y_sb[:]), inc="ltap",
                        after=[e_y_done]))
                    S.emit(blk, sems)

            with (
                nc.sbuf_tensor("x2", [128, 2 * 2048], F32) as x2,
                nc.sbuf_tensor("x3f", [128, 16 * 1024], BF16) as x3f,
                nc.sbuf_tensor("x3tok", [128, 2 * 2048], BF16) as x3tok,
                nc.psum_tensor("pst2", [128, 1024], BF16) as pst,
            ):
                e_x3evac = {}
                x3state = {"prev_tr": None, "last_tr3": None}
                with contextlib.ExitStack() as ctxO:
                    EO = ctxO.enter_context
                    wob = EO(nc.sbuf_tensor("wob", [128, 4 * 8192], BF16))
                    bo_sb = EO(nc.sbuf_tensor("bo_sb", [1, 2048], BF16))
                    onecol = EO(nc.sbuf_tensor("onecol", [1, 128], BF16))
                    stage2 = EO(nc.sbuf_tensor("stage2", [128, 2 * 2048], F32))
                    xsq2 = EO(nc.sbuf_tensor("xsq2", [128, 512], F32))
                    st2s = EO(nc.sbuf_tensor("st2s", [128, 16], F32))
                    sacc = EO(nc.sbuf_tensor("sacc", [128, 4], F32))
                    blk = EO(nc.Block())
                    e_bo = S.op("gpsimd", lambda g: g.dma_start(
                        out=bo_sb[:], in_=bo_in), inc="lbo", after=e_taps)
                    e_one = S.op("vector", lambda v: v.memset(onecol[:], 1.0),
                                 inc="ve")
                    e_wo_d = {}
                    for dcn in range(4):
                        e_wo_d[dcn] = S.op("gpsimd", lambda g, dcn=dcn:
                                           g.dma_start(
                            out=wob[:, dcn * 8192:(dcn + 1) * 8192].rearrange(
                                "p (fc n) -> p fc n", fc=16),
                            in_=wo_img[dcn]),
                            inc=("lwo", "lwo1", "lwv", "lwv1")[dcn],
                            after=(e_taps if dcn == 0 else []))
                    sred_last = {}
                    ssq_last = {}
                    e_x2done = {}
                    x2_free = {}
                    x3tr = {0: None, 1: None}
                    prev_tr = {"tr": None}
                    opsus = [ps2[:], ps3[:], ps4[:], ps0[:, 0:512],
                             ps0[:, 512:1024], ps1[:, 0:512],
                             ps1[:, 512:1024]]
                    tr3_pend = []
                    for tt in range(8):
                        par = tt % 2
                        sb = 8 * par
                        x2t = x2[:, par * 2048:(par + 1) * 2048]
                        x3t = x3tok[:, par * 2048:(par + 1) * 2048]
                        st2 = stage2[:, par * 2048:(par + 1) * 2048]
                        free_pre = x2_free.get(tt - 2, [])
                        sacm = {}
                        e_ld2 = S.op("sync", lambda sy, tt=tt, st2=st2:
                                     sy.dma_start(out=st2,
                                                  in_=x_in[1024 + tt * 128:
                                                           1024 + (tt + 1) * 128, :]),
                                     inc=("lo0", "lo1")[par],
                                     after=[e_x2done.get(tt - 2)] +
                                     (e_taps if tt < 2 else []))
                        for dcn in range(4):
                            pid = (tt * 4 + dcn) % 7
                            psu = opsus[pid]
                            for fc in range(16):
                                def mmo1(t, fc=fc, tt=tt, psu=psu, dcn=dcn):
                                    return t.matmul(
                                        psu,
                                        y_sb[:, fc * 1024 + tt * 128:
                                             fc * 1024 + (tt + 1) * 128],
                                        wob[:, dcn * 8192 + fc * 512:
                                            dcn * 8192 + (fc + 1) * 512],
                                        start=(fc == 0), stop=False)
                                e_mm = S.op("tensor", mmo1, inc="te",
                                            after=[e_wo_d[dcn],
                                                   pair_ymuls.get(fc),
                                                   (psu_last.get(pid)
                                                    if fc == 0 else None)] +
                                            (e_taps if fc == 0 else []))
                            def mmob(t, dcn=dcn, psu=psu):
                                return t.matmul(psu, onecol[:],
                                                bo_sb[:, dcn * 512:(dcn + 1) * 512],
                                                start=False, stop=True)
                            e_mm = S.op("tensor", mmob, inc="te",
                                        after=[e_bo, e_one])
                            ev = S.op("scalar",
                                      lambda s, dcn=dcn, x2t=x2t, psu=psu:
                                      s.copy(x2t[:, dcn * 512:(dcn + 1) * 512],
                                             psu[:]), inc="se",
                                      after=[e_mm] + free_pre)
                            psu_last[pid] = ev
                            sl = (tt * 4 + dcn) % 2
                            x2c = x2t[:, dcn * 512:(dcn + 1) * 512]
                            e_red = S.op("vector", lambda v, x2c=x2c, sl=sl:
                                         v.tensor_reduce(
                                sacc[:, sl:sl + 1], x2c, AX, ALU.add),
                                inc="ve", after=[ev, sred_last.get(sl)])
                            if dcn == 0:
                                e_a1 = S.op("vector", lambda v, tt=tt, sl=sl:
                                            v.tensor_copy(
                                    st2s[:, 2 * tt:2 * tt + 1],
                                    sacc[:, sl:sl + 1]), inc="ve",
                                    after=[e_red])
                            else:
                                e_a1 = S.op("vector", lambda v, tt=tt, sl=sl:
                                            v.tensor_tensor(
                                    st2s[:, 2 * tt:2 * tt + 1],
                                    st2s[:, 2 * tt:2 * tt + 1],
                                    sacc[:, sl:sl + 1], ALU.add), inc="ve",
                                    after=[e_red, sacm[(tt, 0)]])
                            sred_last[sl] = e_a1
                            sacm[(tt, 0)] = e_a1
                            e_sq = S.op("scalar", lambda s, x2c=x2c, sl=sl:
                                        s.activation(
                                xsq2[:, 0:512], x2c, AF.Square,
                                accum_out=sacc[:, 2 + sl:3 + sl]), inc="se",
                                after=[ev, ssq_last.get(sl)])
                            if dcn == 0:
                                e_a2 = S.op("vector", lambda v, tt=tt, sl=sl:
                                            v.tensor_copy(
                                    st2s[:, 2 * tt + 1:2 * tt + 2],
                                    sacc[:, 2 + sl:3 + sl]), inc="ve",
                                    after=[e_sq])
                            else:
                                e_a2 = S.op("vector", lambda v, tt=tt, sl=sl:
                                            v.tensor_tensor(
                                    st2s[:, 2 * tt + 1:2 * tt + 2],
                                    st2s[:, 2 * tt + 1:2 * tt + 2],
                                    sacc[:, 2 + sl:3 + sl], ALU.add),
                                    inc="ve",
                                    after=[e_sq, sacm[(tt, 1)]])
                            ssq_last[sl] = e_a2
                            sacm[(tt, 1)] = e_a2
                        if len(tr3_pend) >= 2:
                            tr3_pend.pop(0)()
                        pre_p = [sacm[(tt, 0)], sacm[(tt, 1)]]
                        c1 = S.op("vector", lambda v, tt=tt, sb=sb:
                                  v.tensor_scalar(
                            stats[:, sb + 2:sb + 3], st2s[:, 2 * tt:2 * tt + 1],
                            1.0 / D, None, ALU.mult), inc="ve", after=pre_p)
                        c2 = S.op("vector", lambda v, sb=sb: v.tensor_tensor(
                            stats[:, sb + 4:sb + 5], stats[:, sb + 2:sb + 3],
                            stats[:, sb + 2:sb + 3],
                            ALU.mult), inc="ve", after=[c1])
                        c3 = S.op("vector", lambda v, tt=tt, sb=sb:
                                  v.tensor_scalar(
                            stats[:, sb + 3:sb + 4],
                            st2s[:, 2 * tt + 1:2 * tt + 2],
                            1.0 / D, None, ALU.mult), inc="ve", after=pre_p)
                        c4 = S.op("vector", lambda v, sb=sb: v.tensor_tensor(
                            stats[:, sb + 3:sb + 4], stats[:, sb + 3:sb + 4],
                            stats[:, sb + 4:sb + 5],
                            ALU.subtract), inc="ve", after=[c2, c3])
                        c5 = S.op("vector", lambda v, sb=sb: v.tensor_scalar(
                            stats[:, sb + 3:sb + 4], stats[:, sb + 3:sb + 4],
                            EPS, None, ALU.add), inc="ve", after=[c4])
                        e_ch = S.op("vector", lambda v, sb=sb: v.reciprocal(
                            stats[:, sb + 5:sb + 6], stats[:, sb + 3:sb + 4]),
                            inc="ve", after=[c5])
                        e_rstd = S.op("scalar", lambda s, sb=sb: s.activation(
                            stats[:, sb + 6:sb + 7], stats[:, sb + 5:sb + 6],
                            AF.Sqrt), inc="se", after=[e_ch])
                        pre = [c1, e_ch, e_rstd]
                        e_ap2 = S.op("vector", lambda v, x2t=x2t, sb=sb:
                                     v.tensor_scalar(x2t, x2t,
                                                     stats[:, sb + 2:sb + 3],
                                                     stats[:, sb + 6:sb + 7],
                                                     ALU.subtract, ALU.mult),
                                     inc="ve", after=pre)
                        e_res = S.op("vector", lambda v, x2t=x2t, st2=st2:
                                     v.tensor_tensor(x2t, x2t, st2, ALU.add),
                                     inc="ve", after=[e_ap2, e_ld2])
                        e_x2done[tt] = e_res
                        e_st = S.op("sync", lambda sy, tt=tt, x2t=x2t:
                                    sy.dma_start(
                            out=out_x2[tt * 128:(tt + 1) * 128, :], in_=x2t),
                            inc=("st", "st1")[par], after=[e_res])
                        e3_sum = S.op("vector", lambda v, x2t=x2t, sb=sb:
                                      v.tensor_reduce(stats[:, sb:sb + 1], x2t,
                                                      AX, ALU.add), inc="ve",
                                      after=[e_res])
                        e3_sq = S.op("scalar", lambda sc, x2t=x2t, x3t=x3t,
                                     sb=sb: sc.activation(
                            x3t, x2t, AF.Square,
                            accum_out=stats[:, sb + 1:sb + 2]), inc="se",
                            after=[e_res, x3tr[par]])
                        c31 = S.op("vector", lambda v, sb=sb: v.tensor_scalar(
                            stats[:, sb + 2:sb + 3], stats[:, sb:sb + 1],
                            1.0 / D, None,
                            ALU.mult), inc="ve", after=[e3_sum])
                        c32 = S.op("vector", lambda v, sb=sb: v.tensor_tensor(
                            stats[:, sb + 4:sb + 5], stats[:, sb + 2:sb + 3],
                            stats[:, sb + 2:sb + 3],
                            ALU.mult), inc="ve", after=[c31])
                        c33 = S.op("vector", lambda v, sb=sb: v.tensor_scalar(
                            stats[:, sb + 3:sb + 4], stats[:, sb + 1:sb + 2],
                            1.0 / D, None,
                            ALU.mult), inc="ve", after=[e3_sq])
                        c34 = S.op("vector", lambda v, sb=sb: v.tensor_tensor(
                            stats[:, sb + 3:sb + 4], stats[:, sb + 3:sb + 4],
                            stats[:, sb + 4:sb + 5],
                            ALU.subtract), inc="ve", after=[c32, c33])
                        c35 = S.op("vector", lambda v, sb=sb: v.tensor_scalar(
                            stats[:, sb + 3:sb + 4], stats[:, sb + 3:sb + 4],
                            EPS, None, ALU.add),
                            inc="ve", after=[c34])
                        e3_ch = S.op("vector", lambda v, sb=sb: v.reciprocal(
                            stats[:, sb + 5:sb + 6], stats[:, sb + 3:sb + 4]),
                            inc="ve", after=[c35])
                        e3_rstd = S.op("scalar", lambda s, sb=sb: s.activation(
                            stats[:, sb + 6:sb + 7], stats[:, sb + 5:sb + 6],
                            AF.Sqrt), inc="se", after=[e3_ch])
                        pre3 = [c31, e3_ch, e3_rstd]
                        e_ap3 = S.op("vector", lambda v, x2t=x2t, x3t=x3t,
                                     sb=sb: v.tensor_scalar(
                                         x3t, x2t,
                                         stats[:, sb + 2:sb + 3],
                                         stats[:, sb + 6:sb + 7],
                                         ALU.subtract, ALU.mult),
                                     inc="ve",
                                     after=pre3 + [x3tr[par]])

                        def emit_tr3(tt=tt, par=par, x3t=x3t, e_ap3=e_ap3):
                            for h in range(2):
                                def trs(t, h=h, x3t=x3t):
                                    out = []
                                    for dci in range(8):
                                        dc = h * 8 + dci
                                        out.append(t.transpose(
                                            pst[:, dci * 128:(dci + 1) * 128],
                                            x3t[:, dc * 128:(dc + 1) * 128],
                                            idn[:]))
                                    return out
                                e_tr = S.op("tensor", trs, inc="te", n=8,
                                            after=[e_ap3, prev_tr["tr"]])
                                def evac3(vv, tt=tt, h=h):
                                    src = pst[:, 0:1024].rearrange(
                                        "p (dc t) -> p dc t", dc=8)
                                    dst = x3f[:].rearrange(
                                        "p (dc t) -> p dc t", dc=16)[
                                            :, h * 8:(h + 1) * 8,
                                            tt * 128:(tt + 1) * 128]
                                    return vv.tensor_copy(dst, src)
                                ev3 = S.op("vector", evac3, inc="ve",
                                           after=[e_tr])
                                e_x3evac[(tt, h)] = ev3
                                prev_tr["tr"] = ev3
                                x3tr[par] = e_tr
                        tr3_pend.append(emit_tr3)
                        x2_free[tt] = [e_ap3, e3_sq, e_st]
                    for fn in tr3_pend:
                        fn()
                    del tr3_pend[:]
                    fence = S.emit(blk, sems)
                ys_stack.close()

                with contextlib.ExitStack() as ctxF:
                    EF = ctxF.enter_context
                    h1 = EF(nc.sbuf_tensor("h1", [128, 16 * 1024], BF16))
                    w1b = EF(nc.sbuf_tensor("w1b", [128, 2 * 2048], BF16))
                    g2c = EF(nc.sbuf_tensor("g2c", [128, 8 * 512], BF16))
                    wfb = EF(nc.sbuf_tensor("wfb", [128, 2 * 8192], BF16))
                    wg = EF(nc.sbuf_tensor("wg", [128, 64], BF16))
                    wrb = EF(nc.sbuf_tensor("wrb", [128, 128], BF16))
                    w2b = EF(nc.sbuf_tensor("w2b", [128, 2 * 2048], BF16))
                    w2f = EF(nc.sbuf_tensor("w2f", [128, 2 * 2048], F32))
                    g2b = EF(nc.sbuf_tensor("g2b", [128, 128], BF16))
                    psout = EF(nc.sbuf_tensor("psout", [8, 2048], F32))
                    rlog = EF(nc.sbuf_tensor("rlog", [128, 8], F32))
                    rt = EF(nc.sbuf_tensor("rt", [128, 24], F32))
                    blk = EF(nc.Block())
                    e_x3f = [e_x3evac[(tt, h)] for tt in range(8) for h in range(2)]

                    if "t_x3f" in taps:
                        S.op("gpsimd", lambda g: g.dma_start(
                            out=taps["t_x3f"], in_=x3f[:]), inc="ltap",
                            after=e_x3f)

                    e_wr = S.op("gpsimd", lambda g: g.dma_start(
                        out=wrb[:].rearrange("p (dc n) -> p dc n", dc=16),
                        in_=wr_img), inc="lwr")
                    e_wg_done = []
                    for tt in range(8):
                        def mmr(t, tt=tt):
                            for dc in range(16):
                                ins = t.matmul(ps4[0:128, 0:8],
                                               x3f[:, dc * 1024 + tt * 128:
                                                   dc * 1024 + (tt + 1) * 128],
                                               wrb[:, dc * 8:(dc + 1) * 8],
                                               start=(dc == 0), stop=(dc == 15))
                            return ins
                        e_mr = S.op("tensor", mmr, inc="te",
                                    after=[e_wr, psu_last.get(2),
                                           e_x3evac[(tt, 0)], e_x3evac[(tt, 1)]])

                        e_g0 = S.op("vector", lambda v: v.tensor_reduce(
                            rt[:, 0:1], ps4[0:128, 0:8], AX, ALU.max),
                            inc="ve", after=[e_mr])
                        e_g1 = S.op("vector", lambda v: v.tensor_scalar(
                            rt[:, 1:2], rt[:, 0:1], -1.0, None, ALU.mult),
                            inc="ve", after=[e_g0])
                        e_g2 = S.op("scalar", lambda s: s.activation(
                            rlog[:], ps4[0:128, 0:8], AF.Exp,
                            bias=rt[:, 1:2]), inc="se", after=[e_g1])
                        psu_last[2] = e_g2

                        q1 = S.op("vector", lambda v: v.tensor_scalar(
                            rt[:, 2:10], rlog[:], 1.0, None, ALU.is_ge),
                            inc="ve", after=[e_g2])
                        q2 = S.op("vector", lambda v: v.tensor_tensor(
                            rt[:, 2:10], rlog[:], rt[:, 2:10], ALU.subtract),
                            inc="ve", after=[q1])
                        q3 = S.op("vector", lambda v: v.tensor_reduce(
                            rt[:, 10:11], rt[:, 2:10], AX, ALU.max),
                            inc="ve", after=[q2])
                        q4 = S.op("vector", lambda v: v.tensor_scalar(
                            rt[:, 11:19], rlog[:], rt[:, 10:11], None,
                            ALU.is_ge), inc="ve", after=[q3])
                        q5 = S.op("vector", lambda v: v.tensor_tensor(
                            rt[:, 11:19], rlog[:], rt[:, 11:19], ALU.mult),
                            inc="ve", after=[q4])
                        q6 = S.op("vector", lambda v: v.tensor_scalar(
                            rt[:, 10:11], rt[:, 10:11], 1.0, None, ALU.add),
                            inc="ve", after=[q4])
                        q7 = S.op("vector", lambda v: v.reciprocal(
                            rt[:, 20:21], rt[:, 10:11]), inc="ve", after=[q6])
                        q8 = S.op("vector", lambda v, tt=tt: v.tensor_scalar(
                            wg[:, tt * 8:(tt + 1) * 8], rt[:, 11:19],
                            rt[:, 20:21], None, ALU.mult), inc="ve",
                            after=[q5, q7])
                        e_wg_done.append(q8)
                    if "t_wg" in taps:
                        S.op("gpsimd", lambda g: g.dma_start(
                            out=taps["t_wg"], in_=wg[:]), inc="ltap",
                            after=e_wg_done)

                    e_h1 = {}
                    w1_users = {0: [], 1: []}
                    e_w1_d = {}

                    def emit_w1_load(ic):
                        wp = ic % 2
                        e_w1_d[ic] = S.op("gpsimd", lambda g, ic=ic, wp=wp:
                                          g.dma_start(
                            out=w1b[:, wp * 2048:(wp + 1) * 2048],
                            in_=w1_img[ic]), inc=("lw1", "lw11")[wp],
                            after=list(w1_users[wp]))
                        w1_users[wp] = []
                    emit_w1_load(0)
                    emit_w1_load(1)
                    e_w1_users = []
                    for ic in range(16):
                        wp = ic % 2
                        e_w1 = e_w1_d[ic]
                        for tc in range(2):
                            pid = tc
                            psu = (ps2, ps3)[pid]
                            def mm1(t, ic=ic, tc=tc, psu=psu, wp=wp):
                                for dc in range(16):
                                    ins = t.matmul(
                                        psu[:],
                                        w1b[:, wp * 2048 + dc * 128:
                                            wp * 2048 + (dc + 1) * 128],
                                        x3f[:, dc * 1024 + tc * 512:
                                            dc * 1024 + (tc + 1) * 512],
                                        start=(dc == 0), stop=(dc == 15))
                                return ins
                            e_mm = S.op("tensor", mm1, inc="te",
                                        after=[e_w1, psu_last.get(pid)] +
                                        [e_x3evac[(4 * tc + i, hh)]
                                         for i in range(4) for hh in range(2)])
                            w1_users[wp].append(e_mm)
                            e_w1_users.append(e_mm)
                            ev = S.op("scalar", lambda s, ic=ic, tc=tc, psu=psu:
                                      s.activation(h1[:, ic * 1024 + tc * 512:
                                                      ic * 1024 + (tc + 1) * 512],
                                                   psu[:], AF.Silu),
                                      inc="se", after=[e_mm])
                            psu_last[pid] = ev
                            e_h1[(ic, tc)] = ev
                        if tc == 1 and ic + 2 < 16:
                            emit_w1_load(ic + 2)
                    e_h1_all = list(e_h1.values())
                    if "t_h1" in taps:
                        S.op("gpsimd", lambda g: g.dma_start(
                            out=taps["t_h1"], in_=h1[:]), inc="ltap",
                            after=e_h1_all)

                    e_agg = {}
                    w3_users = {0: [], 1: []}
                    e_w3_d = {}

                    def emit_w3_load(i2c):
                        wp = i2c % 2
                        e_w3_d[i2c] = S.op("gpsimd", lambda g, i2c=i2c, wp=wp:
                                           g.dma_start(
                            out=wfb[:, wp * 8192:(wp + 1) * 8192].rearrange(
                                "p (ic n) -> p ic n", ic=16),
                            in_=w3_img[i2c]), inc=("lw3", "lw31")[wp],
                            after=list(w3_users[wp]))
                        w3_users[wp] = []
                    emit_w3_load(0)
                    emit_w3_load(1)
                    w2_users = {0: [], 1: []}
                    e_w2_d = {}
                    e_w2_cast = {}

                    def emit_w2_load(i2s):
                        wp = i2s % 2
                        e_w2_d[i2s] = S.op("sync", lambda sy, i2s=i2s, wp=wp:
                                           sy.dma_start(
                            out=w2f[:, wp * 2048:(wp + 1) * 2048],
                            in_=w2_img[i2s]), inc=("lw2a", "lw2b")[wp],
                            after=[e_w2_cast.get(i2s - 2)])
                        w2_users[wp] = []
                    emit_w2_load(0)
                    emit_w2_load(1)
                    e_mmw_last = {}
                    for i2c in range(4):
                        wp = i2c % 2
                        e_w3 = e_w3_d[i2c]
                        e_g2c = {}
                        for tt in range(8):
                            pid = tt % 2
                            psu = (ps2, ps3)[pid]
                            def mm2(t, i2c=i2c, tt=tt, psu=psu, wp=wp):
                                for ic in range(16):
                                    ins = t.matmul(
                                        psu[:],
                                        h1[:, ic * 1024 + tt * 128:
                                           ic * 1024 + (tt + 1) * 128],
                                        wfb[:, wp * 8192 + ic * 512:
                                            wp * 8192 + (ic + 1) * 512],
                                        start=(ic == 0), stop=(ic == 15))
                                return ins
                            e_mm = S.op("tensor", mm2, inc="te",
                                        after=[e_w3, psu_last.get(pid),
                                               e_g2c.get(tt - 2)] + e_h1_all)
                            w3_users[wp].append(e_mm)
                            ev = S.op("scalar", lambda s, tt=tt, psu=psu:
                                      s.activation(g2c[:, tt * 512:(tt + 1) * 512],
                                                   psu[:], AF.Silu),
                                      inc="se", after=[e_mm])
                            psu_last[pid] = ev
                            e_g2c[tt] = ev
                        if i2c + 2 < 4:
                            emit_w3_load(i2c + 2)
                        for i2q in range(4):
                            i2s = i2c * 4 + i2q
                            last = None
                            for tt in range(8):
                                def mma(t, i2s=i2s, i2q=i2q, tt=tt):
                                    return t.matmul(
                                        ps4[0:128, i2s * 8 % 512:
                                            i2s * 8 % 512 + 8]
                                        if False else
                                        ps4[0:128, (i2s % 16) * 8:
                                            (i2s % 16) * 8 + 8],
                                        g2c[:, tt * 512 + i2q * 128:
                                            tt * 512 + (i2q + 1) * 128],
                                        wg[:, tt * 8:(tt + 1) * 8],
                                        start=(tt == 0), stop=(tt == 7))
                                pre = [e_g2c[tt]] + e_wg_done
                                if tt == 0:
                                    pre.append(psu_last.get(2))
                                last = S.op("tensor", mma, inc="te", after=pre)
                            ev = S.op("vector", lambda v, i2s=i2s: v.tensor_copy(
                                gsb[:, i2s * 8:(i2s + 1) * 8],
                                ps4[0:128, (i2s % 16) * 8:(i2s % 16) * 8 + 8]),
                                inc="ve", after=[last])
                            e_agg[i2s] = ev
                            e_cb = S.op("vector", lambda v, i2s=i2s:
                                        v.tensor_copy(
                                g2b[:, i2s * 8:(i2s + 1) * 8],
                                gsb[:, i2s * 8:(i2s + 1) * 8]),
                                inc="ve", after=[ev])
                            e_w2_cast[i2s] = S.op(
                                "vector", lambda v, i2s=i2s: v.tensor_copy(
                                    w2b[:, (i2s % 2) * 2048:
                                        (i2s % 2 + 1) * 2048],
                                    w2f[:, (i2s % 2) * 2048:
                                        (i2s % 2 + 1) * 2048]),
                                inc="ve",
                                after=[e_w2_d[i2s],
                                       e_mmw_last.get(3, None)
                                       if i2s >= 2 else None])
                            for dcn in range(4):
                                psu_ap = (ps0, ps0, ps1, ps1)[dcn][
                                    0:8, (dcn % 2) * 512:(dcn % 2) * 512 + 512]
                                def mmw(t, i2s=i2s, dcn=dcn, psu_ap=psu_ap):
                                    return t.matmul(
                                        psu_ap,
                                        g2b[:, i2s * 8:(i2s + 1) * 8],
                                        w2b[:, (i2s % 2) * 2048 + dcn * 512:
                                            (i2s % 2) * 2048 + (dcn + 1) * 512],
                                        start=(i2s == 0), stop=(i2s == 15))
                                pre_w = [e_cb, e_w2_cast[i2s]]
                                if i2s == 0:
                                    pre_w.append(psu_last.get(3 + dcn))
                                e_mmw_last[dcn] = S.op("tensor", mmw,
                                                       inc="te", after=pre_w)
                            if i2s + 2 < 16:
                                emit_w2_load(i2s + 2)
                        psu_last[2] = e_agg[i2c * 4 + 3]
                    e_po0 = S.op("vector", lambda v: v.tensor_copy(
                        psout[0:8, 0:1024], ps0[0:8, 0:1024]), inc="ve",
                        after=[e_mmw_last[0], e_mmw_last[1]])
                    e_po1 = S.op("vector", lambda v: v.tensor_copy(
                        psout[0:8, 1024:2048], ps1[0:8, 0:1024]), inc="ve",
                        after=[e_mmw_last[2], e_mmw_last[3]])
                    S.op("sync", lambda sy: sy.dma_start(
                        out=out_ps, in_=psout[:]), inc="st",
                        after=[e_po0, e_po1])
                    S.op("sync", lambda sy: sy.dma_start(
                        out=out_g, in_=gsb[:]), inc="st",
                        after=list(e_agg.values()))
                    st_total = S.counts.get("st", 0)
                    st1_total = S.counts.get("st1", 0)
                    ltap_total = S.counts.get("ltap", 0)
                    S.op("sync", lambda sy: sy.wait_ge(sems["st"], st_total))
                    if st1_total:
                        S.op("sync", lambda sy: sy.wait_ge(sems["st1"],
                                                           st1_total))
                    if ltap_total:
                        S.op("gpsimd", lambda g: g.wait_ge(sems["ltap"],
                                                           ltap_total))
                    fence = S.emit(blk, sems)

    return nc, taps


def build_fix():
    nc = bass.Bass("TRN2", target_bir_lowering=False)
    ap = lambda *a, **kw: nc.dram_tensor(*a, **kw).ap()
    ps_in = ap("ps_in", [128, 2048], F32, kind="ExternalInput")
    sel_in = ap("sel_in", [128, 8], F32, kind="ExternalInput")
    x2r_in = ap("x2r_in", [2, 8, 2048], F32, kind="ExternalInput")
    rows_out = ap("rows_out", [8, 2048], F32, kind="ExternalOutput")

    with contextlib.ExitStack() as ctx:
        E = ctx.enter_context
        sems = {k: E(nc.semaphore(k)) for k in
                ("te", "ve", "se", "lc", "lxa0", "st")}
        S = Sched()
        pf = E(nc.sbuf_tensor("pf", [128, 2048], F32))
        rf = E(nc.sbuf_tensor("rf", [128, 2048], F32))
        pb = E(nc.sbuf_tensor("pb", [128, 2048], BF16))
        rb = E(nc.sbuf_tensor("rb", [128, 2048], BF16))
        selb = E(nc.sbuf_tensor("selb", [128, 8], BF16))
        rows = E(nc.sbuf_tensor("rows", [8, 2048], F32))
        x2r = E(nc.sbuf_tensor("x2r", [8, 2048], F32))
        rsq = E(nc.sbuf_tensor("rsq", [8, 2048], F32))
        stats = E(nc.sbuf_tensor("stats", [8, 16], F32))
        psA = E(nc.psum_tensor("psA", [8, 2048], F32))

        with nc.Block() as blk:
            e_ld = S.op("gpsimd", lambda g: g.dma_start(out=pf[:],
                                                        in_=ps_in), inc="lc")
            e_sl = S.op("gpsimd", lambda g: g.dma_start(out=selb[:],
                                                        in_=sel_in), inc="lc",
                        after=[e_ld])
            e_x2r = S.op("gpsimd", lambda g: g.dma_start(
                out=x2r[:], in_=x2r_in[0]), inc="lxa0")
            e_x2rl = S.op("gpsimd", lambda g: g.dma_start(
                out=rsq[:], in_=x2r_in[1]), inc="lxa0", after=[e_x2r])
            e_x2a = S.op("vector", lambda v: v.tensor_tensor(
                x2r[:], x2r[:], rsq[:], ALU.add), inc="ve",
                after=[e_x2r, e_x2rl])
            e_x2r = e_x2a
            e_hi = S.op("vector", lambda v: v.tensor_copy(pb[:], pf[:]),
                        inc="ve", after=[e_ld])
            e_r0 = S.op("vector", lambda v: v.tensor_copy(rf[:], pb[:]),
                        inc="ve", after=[e_hi])
            e_r1 = S.op("vector", lambda v: v.tensor_tensor(
                rf[:], pf[:], rf[:], ALU.subtract), inc="ve", after=[e_r0])
            e_rb = S.op("vector", lambda v: v.tensor_copy(rb[:], rf[:]),
                        inc="ve", after=[e_r1])
            mm_last = None
            for dcn in range(4):
                def mma(t, dcn=dcn):
                    return t.matmul(
                        psA[:, dcn * 512:(dcn + 1) * 512], selb[:],
                        pb[:, dcn * 512:(dcn + 1) * 512],
                        start=True, stop=False)
                def mmb(t, dcn=dcn):
                    return t.matmul(
                        psA[:, dcn * 512:(dcn + 1) * 512], selb[:],
                        rb[:, dcn * 512:(dcn + 1) * 512],
                        start=False, stop=True)
                S.op("tensor", mma, inc="te", after=[e_hi, e_sl])
                mm_last = S.op("tensor", mmb, inc="te", after=[e_rb])
            e_ev = S.op("vector", lambda v: v.tensor_copy(rows[:], psA[:]),
                        inc="ve", after=[mm_last])
            e_sum = S.op("vector", lambda v: v.tensor_reduce(
                stats[:, 0:1], rows[:], AX, ALU.add), inc="ve", after=[e_ev])
            e_sq = S.op("scalar", lambda s: s.activation(
                rsq[:], rows[:], AF.Square, accum_out=stats[:, 1:2]),
                inc="se", after=[e_ev, e_x2a])

            c1 = S.op("vector", lambda v: v.tensor_scalar(
                stats[:, 2:3], stats[:, 0:1], 1.0 / D, None, ALU.mult),
                inc="ve", after=[e_sum])
            c2 = S.op("vector", lambda v: v.tensor_tensor(
                stats[:, 4:5], stats[:, 2:3], stats[:, 2:3], ALU.mult),
                inc="ve", after=[c1])
            c3 = S.op("vector", lambda v: v.tensor_scalar(
                stats[:, 3:4], stats[:, 1:2], 1.0 / D, None, ALU.mult),
                inc="ve", after=[e_sq])
            c4 = S.op("vector", lambda v: v.tensor_tensor(
                stats[:, 3:4], stats[:, 3:4], stats[:, 4:5], ALU.subtract),
                inc="ve", after=[c2, c3])
            c5 = S.op("vector", lambda v: v.tensor_scalar(
                stats[:, 3:4], stats[:, 3:4], EPS, None, ALU.add),
                inc="ve", after=[c4])
            e_ch = S.op("vector", lambda v: v.reciprocal(
                stats[:, 5:6], stats[:, 3:4]), inc="ve", after=[c5])
            e_rstd = S.op("scalar", lambda s: s.activation(
                stats[:, 6:7], stats[:, 5:6], AF.Sqrt), inc="se", after=[e_ch])
            e_ap = S.op("vector", lambda v: v.tensor_scalar(
                rows[:], rows[:], stats[:, 2:3], stats[:, 6:7],
                ALU.subtract, ALU.mult), inc="ve", after=[e_rstd])
            e_fin = S.op("vector", lambda v: v.tensor_tensor(
                rows[:], rows[:], x2r[:], ALU.add), inc="ve",
                after=[e_ap, e_x2r])
            S.op("sync", lambda sy: sy.dma_start(out=rows_out, in_=rows[:]),
                 inc="st", after=[e_fin])
            stt = S.counts.get("st", 0)
            S.op("sync", lambda sy: sy.wait_ge(sems["st"], stt))
            S.emit(blk, sems)
    return nc


def _host_prep(inputs):
    f32 = np.float32
    x = np.ascontiguousarray(inputs["L0_attn_hidden_states"], dtype=f32)
    Wq = np.ascontiguousarray(inputs["L0_attn_Wq"], dtype=f32)
    Wk = np.ascontiguousarray(inputs["L0_attn_Wk"], dtype=f32)
    Wv = np.ascontiguousarray(inputs["L0_attn_Wv"], dtype=f32)
    Wo = np.ascontiguousarray(inputs["L0_attn_Wo"], dtype=f32)
    bo = np.ascontiguousarray(inputs["L0_attn_bo"], dtype=f32)
    w1 = np.ascontiguousarray(inputs["L0_ffn_w1"], dtype=f32)
    w3 = np.ascontiguousarray(inputs["L0_ffn_w3"], dtype=f32)
    w2 = np.ascontiguousarray(inputs["L0_ffn_w2"], dtype=f32)
    wr = np.ascontiguousarray(inputs["L0_ffn_wr"], dtype=f32)

    shared = {}
    shared["wq_img"] = np.ascontiguousarray(
        Wq.reshape(16, 128, 16, 128).transpose(0, 3, 2, 1)).reshape(16, 128, 2048)
    shared["wk_img"] = np.ascontiguousarray(
        Wk.reshape(16, 128, 16, 128).transpose(0, 3, 2, 1)).reshape(16, 128, 2048)
    shared["wvl_img"] = np.ascontiguousarray(
        Wv[0:1920].reshape(5, 384, 16, 128).transpose(0, 3, 2, 1))
    shared["wvg_img"] = np.ascontiguousarray(
        Wv[1920:2048].reshape(128, 16, 128).transpose(2, 1, 0))
    shared["wo_img"] = np.ascontiguousarray(
        Wo.reshape(4, 512, 16, 128).transpose(0, 3, 2, 1))
    shared["bo_in"] = bo.reshape(1, 2048)
    shared["w1_img"] = np.ascontiguousarray(
        w1.reshape(16, 128, 16, 128).transpose(0, 3, 2, 1)).reshape(16, 128, 2048)
    shared["w3_img"] = np.ascontiguousarray(
        w3.T.reshape(16, 128, 4, 512).transpose(2, 1, 0, 3))
    shared["wr_img"] = np.ascontiguousarray(
        wr.reshape(8, 16, 128).transpose(2, 1, 0))
    shared["tri_in"] = np.ascontiguousarray(
        np.triu(np.ones((128, 128), dtype=f32)))
    shared["idn_in"] = np.eye(128, dtype=f32)
    sel2 = np.zeros((2, 128), dtype=f32)
    sel2[0, 0:64] = 1.0
    sel2[1, 64:128] = 1.0
    shared["sel2_in"] = sel2
    psw = np.zeros((128, 128), dtype=f32)
    for m in range(128):
        psw[m ^ 32, m] = 1.0
    shared["pswap_in"] = psw

    import jax
    import jax.numpy as jnp
    dpat = np.arange(128) % 64
    ipat = dpat % 32
    sgn = np.where(dpat < 32, np.float32(-1.0), np.float32(1.0)).astype(f32)
    with jax.default_device(jax.devices("cpu")[0]):
        _dim = 64
        _invf = 1.0 / 10000.0 ** (jnp.arange(0, _dim, 2, dtype=jnp.float32) / _dim)
        _tabs = {}
        for _h in (0, 1):
            _qe = 1024 * (_h + 1)
            _pos = jnp.arange(_qe - 2048, _qe)
            _fr = _pos[:, None].astype(jnp.float32) * _invf[None, :]
            _tabs[_h] = (np.asarray(jnp.cos(_fr)), np.asarray(jnp.sin(_fr)))

    per_core = []
    for c in range(8):
        b, h = c >> 1, c & 1
        q_end = 1024 * (h + 1)
        _cos32, _sin32 = _tabs[h]
        cosT = _cos32[:, ipat].T.astype(f32)
        sinT = (sgn[None, :] * _sin32[:, ipat]).T.astype(f32)
        if h == 0:
            x_c = np.concatenate(
                [np.zeros((1024, D), dtype=f32), x[b, 0:1024]], axis=0)
        else:
            x_c = x[b]
        bias = np.full((128, 56), -CSTAB, dtype=f32)
        bs_loc = 0 if h == 0 else 512
        for qc in range(2):
            for kt in range(LOC_NKT[qc]):
                ka = (np.arange(128) + kt * 128) + (q_end - 1536)
                m = ka < max(0, bs_loc)
                bias[m, qc * 12 + kt] = NEGBIG
            for kt in range(GLB_NKT[qc]):
                ka = (np.arange(128) + kt * 128) + (q_end - 2048)
                m = ka < 0
                bias[m, 24 + qc * 16 + kt] = NEGBIG
        per_core.append(dict(x_in=np.ascontiguousarray(x_c),
                             cos_in=np.ascontiguousarray(cosT),
                             sin_in=np.ascontiguousarray(sinT),
                             biasv_in=np.ascontiguousarray(bias)))

    w2_img = np.ascontiguousarray(w2.T.reshape(16, 128, 2048))
    shared["w2_img"] = w2_img
    return shared, per_core, w2_img


def kernel(**inputs):
    shared, per_core, w2_img = _host_prep(inputs)
    nc, _ = build_main()
    in_maps = []
    for c in range(8):
        m = dict(shared)
        m.update(per_core[c])
        in_maps.append(m)
    res = run_bass_kernel_spmd(nc, in_maps, core_ids=list(range(8)))

    outs = [res.results[c] for c in range(8)]
    pss = np.stack([o["out_ps"] for o in outs], axis=0)
    x2rows = np.ascontiguousarray(outs[0]["out_x2"][0:8, :])

    nc2 = build_fix()
    ps64 = pss.astype(np.float64)
    ps_hi = ps64.astype(np.float32)
    ps_lo = (ps64 - ps_hi).astype(np.float32)
    ps_in = np.ascontiguousarray(
        np.concatenate([ps_hi, ps_lo], axis=0).reshape(128, 2048))
    selm = np.zeros((128, 8), dtype=np.float32)
    selm[np.arange(128), np.arange(128) % 8] = 1.0
    x264 = x2rows.astype(np.float64)
    x2_hi = x264.astype(np.float32)
    x2_lo = (x264 - x2_hi).astype(np.float32)
    res2 = run_bass_kernel_spmd(
        nc2, [dict(ps_in=ps_in, sel_in=selm,
                   x2r_in=np.stack([x2_hi, x2_lo], axis=0))], core_ids=[0])
    rows = res2.results[0]["rows_out"]

    out = np.empty((B, L, D), dtype=np.float32)
    for c in range(8):
        b, h = c >> 1, c & 1
        out[b, h * 1024:(h + 1) * 1024, :] = outs[c]["out_x2"]
    out[0, 0:8, :] = rows
    return out


# version 21 (final)
# speedup vs baseline: 1.0228x; 1.0228x over previous
import contextlib

import numpy as np

import concourse.bass as bass
import concourse.mybir as mybir
from concourse.bass_utils import run_bass_kernel_spmd

F32 = mybir.dt.float32
BF16 = mybir.dt.bfloat16
AF = mybir.ActivationFunctionType
ALU = mybir.AluOpType
AX = mybir.AxisListType.X

B, L, D = 4, 2048, 2048
INTER = 2048
CSTAB = 16.0
NEGBIG = -1.0e30
EPS = 1e-5

LOC_NKT = (8, 12)
LOC_DS = (4, 8)
GLB_NKT = (12, 16)
GLB_DS = (8, 12)

ENGS = ("gpsimd", "sync", "tensor", "vector", "scalar")
COMPUTE_SEMS = ("te", "ve", "se")
SEMNAMES = ("te", "ve", "se", "ge", "lc", "st1", "lw2a", "lw2b", "lxa0", "lxa1", "lo0", "lo1",
            "lwq", "lwq1", "lwk", "lwk1", "lwv", "lwv1", "lwo", "lwo1", "lbo", "lw1", "lw11", "lw3", "lw31", "lwr",
            "ltap", "st")


class Sched:

    def __init__(self):
        self.ops = []
        self.counts = {}
        self.val = {}
        self.emitted = 0

    def op(self, eng, fn, inc=None, after=(), n=1):
        ev = len(self.ops)
        if inc is not None:
            amt = (1 if inc in COMPUTE_SEMS else 16) * n
            self.counts[inc] = self.counts.get(inc, 0) + amt
            self.val[ev] = (inc, self.counts[inc])
        flat = []
        for a in after:
            if a is None:
                continue
            if isinstance(a, (list, tuple)):
                flat.extend(x for x in a if x is not None)
            else:
                flat.append(a)
        self.ops.append((eng, fn, inc, flat))
        return ev

    def emit(self, blk, sems, fence=None):
        lo, hi = self.emitted, len(self.ops)
        self.emitted = hi
        for name in ENGS:
            mine = [i for i in range(lo, hi) if self.ops[i][0] == name]
            if not mine:
                continue

            def body(e, name=name, mine=mine):
                if fence:
                    for s, v in fence.items():
                        if v > 0:
                            e.wait_ge(sems[s], v)
                for i in mine:
                    _, fn, inc, after = self.ops[i]
                    need = {}
                    for ev in after:
                        assert ev in self.val, f"event {ev} has no sem inc"
                        s, v = self.val[ev]
                        if need.get(s, 0) < v:
                            need[s] = v
                    for s, v in sorted(need.items()):
                        e.wait_ge(sems[s], v)
                    ins = fn(e)
                    if inc is not None:
                        amt = 1 if inc in COMPUTE_SEMS else 16
                        if isinstance(ins, (list, tuple)):
                            for one in ins:
                                one.then_inc(sems[inc], amt)
                        else:
                            ins.then_inc(sems[inc], amt)

            getattr(blk, name)(body)
        return dict(self.counts)


def build_main(debug_taps=()):
    nc = bass.Bass("TRN2", target_bir_lowering=False)
    ap = lambda *a, **kw: nc.dram_tensor(*a, **kw).ap()

    x_in = ap("x_in", [L, D], F32, kind="ExternalInput")
    wq_img = ap("wq_img", [16, 128, 2048], F32, kind="ExternalInput")
    wk_img = ap("wk_img", [16, 128, 2048], F32, kind="ExternalInput")
    wvl_img = ap("wvl_img", [5, 128, 16, 384], F32, kind="ExternalInput")
    wvg_img = ap("wvg_img", [128, 16, 128], F32, kind="ExternalInput")
    wo_img = ap("wo_img", [4, 128, 16, 512], F32, kind="ExternalInput")
    bo_in = ap("bo_in", [1, 2048], F32, kind="ExternalInput")
    w1_img = ap("w1_img", [16, 128, 2048], F32, kind="ExternalInput")
    w3_img = ap("w3_img", [4, 128, 16, 512], F32, kind="ExternalInput")
    wr_img = ap("wr_img", [128, 16, 8], F32, kind="ExternalInput")
    cos_in = ap("cos_in", [128, 2048], F32, kind="ExternalInput")
    sin_in = ap("sin_in", [128, 2048], F32, kind="ExternalInput")
    biasv_in = ap("biasv_in", [128, 56], F32, kind="ExternalInput")
    tri_in = ap("tri_in", [128, 128], F32, kind="ExternalInput")
    idn_in = ap("idn_in", [128, 128], F32, kind="ExternalInput")
    sel2_in = ap("sel2_in", [2, 128], F32, kind="ExternalInput")
    pswap_in = ap("pswap_in", [128, 128], F32, kind="ExternalInput")

    w2_img = ap("w2_img", [16, 128, 2048], F32, kind="ExternalInput")
    out_x2 = ap("out_x2", [1024, D], F32, kind="ExternalOutput")
    out_g = ap("out_g", [128, 128], F32, kind="ExternalOutput")
    out_ps = ap("out_ps", [8, 2048], F32, kind="ExternalOutput")
    taps = {}
    tapshapes = {"t_x1f": [128, 32768], "t_y": [128, 16384],
                 "t_h1": [128, 16384], "t_wg": [128, 64],
                 "t_x3f": [128, 16384]}
    for name in debug_taps:
        taps[name] = ap(name, tapshapes[name], F32, kind="ExternalOutput")

    S = Sched()

    with contextlib.ExitStack() as ctx:
        E = ctx.enter_context
        sems = {k: E(nc.semaphore(k)) for k in SEMNAMES}

        biasv = E(nc.sbuf_tensor("biasv", [128, 56], F32))
        tri = E(nc.sbuf_tensor("tri", [128, 128], BF16))
        idn = E(nc.sbuf_tensor("idn", [128, 128], BF16))
        sel2 = E(nc.sbuf_tensor("sel2", [2, 128], BF16))
        pswap = E(nc.sbuf_tensor("pswap", [128, 128], BF16))
        stats = E(nc.sbuf_tensor("stats", [128, 16], F32))
        rab216 = E(nc.sbuf_tensor("rab216", [1, 2048], BF16))

        ps0 = E(nc.psum_tensor("ps0", [128, 1024], F32))
        ps1 = E(nc.psum_tensor("ps1", [128, 1024], F32))
        ps2 = E(nc.psum_tensor("ps2", [128, 512], F32))
        ps3 = E(nc.psum_tensor("ps3", [128, 512], F32))
        ps4 = E(nc.psum_tensor("ps4", [128, 512], F32))

        psu_last = {}

        def ln_stats(src_ap, pre, xsq_ap):
            e_sum = S.op("vector", lambda v: v.tensor_reduce(
                stats[:, 0:1], src_ap, AX, ALU.add), inc="ve", after=pre)
            e_sq = S.op("scalar", lambda s: s.activation(
                xsq_ap, src_ap, AF.Square, accum_out=stats[:, 1:2]), inc="se",
                after=pre)

            c1 = S.op("vector", lambda v: v.tensor_scalar(
                stats[:, 2:3], stats[:, 0:1], 1.0 / D, None, ALU.mult),
                inc="ve", after=[e_sum])
            c2 = S.op("vector", lambda v: v.tensor_tensor(
                stats[:, 4:5], stats[:, 2:3], stats[:, 2:3], ALU.mult),
                inc="ve", after=[c1])
            c3 = S.op("vector", lambda v: v.tensor_scalar(
                stats[:, 3:4], stats[:, 1:2], 1.0 / D, None, ALU.mult),
                inc="ve", after=[e_sq])
            c4 = S.op("vector", lambda v: v.tensor_tensor(
                stats[:, 3:4], stats[:, 3:4], stats[:, 4:5], ALU.subtract),
                inc="ve", after=[c2, c3])
            c5 = S.op("vector", lambda v: v.tensor_scalar(
                stats[:, 3:4], stats[:, 3:4], EPS, None, ALU.add),
                inc="ve", after=[c4])
            e_ch = S.op("vector", lambda v: v.reciprocal(
                stats[:, 5:6], stats[:, 3:4]), inc="ve", after=[c5])
            e_rstd = S.op("scalar", lambda s: s.activation(
                stats[:, 6:7], stats[:, 5:6], AF.Sqrt), inc="se", after=[e_ch])
            return [c1, e_ch, e_rstd]

        pair_ymuls = {}
        d_done = []
        with contextlib.ExitStack() as ctxY:
            ys_stack = contextlib.ExitStack()
            gsb = ctxY.enter_context(nc.sbuf_tensor("gsb", [128, 128], F32))
            with contextlib.ExitStack() as ctxA:
                EA = ctxA.enter_context
                x1f = EA(nc.sbuf_tensor("x1f", [128, 16 * 2048], BF16))
                v65l = EA(nc.sbuf_tensor("v65l", [128, 12 * 1950], BF16))
                v65g = EA(nc.sbuf_tensor("v65g", [128, 16 * 130], BF16))
                cosT = EA(nc.sbuf_tensor("cosT", [128, 2048], BF16))
                sinT = EA(nc.sbuf_tensor("sinT", [128, 2048], BF16))
                def x1f_c(dc, t0, n):
                    return x1f[:, dc * 2048 + t0: dc * 2048 + t0 + n]

                with contextlib.ExitStack() as ctxB:
                    EB = ctxB.enter_context
                    stage = EB(nc.sbuf_tensor("stage", [128, 2 * 2048], F32))
                    xsq = EB(nc.sbuf_tensor("xsq", [128, 2048], F32))
                    x1tok = EB(nc.sbuf_tensor("x1tok", [128, 2 * 2048], BF16))
                    wvbuf = EB(nc.sbuf_tensor("wvbuf", [128, 2 * 6144], BF16))
                    pst = EB(nc.psum_tensor("pst", [128, 1024], BF16))
                    blk = EB(nc.Block())
                    for src, dst in ((cos_in, cosT), (sin_in, sinT),
                                     (biasv_in, biasv), (tri_in, tri),
                                     (idn_in, idn), (sel2_in, sel2),
                                     (pswap_in, pswap)):
                        e_c = S.op("gpsimd", lambda g, s=src, d=dst:
                                   g.dma_start(out=d[:], in_=s), inc="lc")
                    e_consts = e_c
                    e_ones_l = S.op("vector", lambda v: v.memset(
                        v65l[:].rearrange("p (a o) -> p a o", o=65)[:, :, 64:65],
                        1.0), inc="ve")
                    e_ones_g = S.op("vector", lambda v: v.memset(
                        v65g[:].rearrange("p (a o) -> p a o", o=65)[:, :, 64:65],
                        1.0), inc="ve")
                    wv_users = {0: [], 1: []}
                    e_wv_d = {}

                    def emit_wv_load(vc):
                        vp = vc % 2
                        e_wv_d[vc] = S.op("gpsimd", lambda g, vc=vc, vp=vp:
                                          g.dma_start(
                            out=wvbuf[:, vp * 6144:(vp + 1) * 6144].rearrange(
                                "p (dc n) -> p dc n", dc=16),
                            in_=wvl_img[vc]), inc=("lwv", "lwv1")[vp],
                            after=list(wv_users[vp]))
                        wv_users[vp] = []
                    emit_wv_load(0)
                    emit_wv_load(1)

                    vl_pend = []
                    vstate = {"n": 0, "last": None}

                    def emit_vl(vc, tt, ready, defer=False):
                        vp = vc % 2
                        pid = vstate["n"] % 3
                        vstate["n"] += 1
                        psu = (ps2, ps3, ps4)[pid]
                        def mm(t, vc=vc, tt=tt, psu=psu, vp=vp):
                            for dc in range(16):
                                ins = t.matmul(
                                    psu[:, 0:384],
                                    x1f_c(dc, 512 + tt * 128, 128),
                                    wvbuf[:, vp * 6144 + dc * 384:
                                          vp * 6144 + (dc + 1) * 384],
                                    start=(dc == 0), stop=(dc == 15))
                            return ins
                        e_mm = S.op("tensor", mm, inc="te",
                                    after=[e_wv_d[vc], ready,
                                           psu_last.get(pid)])
                        wv_users[vp].append(e_mm)

                        def emit_ev(vc=vc, tt=tt, psu=psu, pid=pid, e_mm=e_mm):
                            def evac(v, vc=vc, tt=tt, psu=psu):
                                src = psu[:, 0:384].rearrange(
                                    "p (h d) -> p h d", d=64)
                                dst = v65l[:, tt * 1950 + vc * 390:
                                           tt * 1950 + vc * 390 + 390].rearrange(
                                    "p (h o) -> p h o", o=65)[:, :, 0:64]
                                return v.tensor_copy(dst, src)
                            ev = S.op("vector", evac, inc="ve", after=[e_mm])
                            psu_last[pid] = ev
                            vstate["last"] = ev
                        if defer:
                            vl_pend.append(emit_ev)
                        else:
                            emit_ev()

                    e_apply = {}
                    e_evac = {}
                    e_tr_last = {}
                    for tt in range(16):
                        par = tt % 2
                        st_ap = stage[:, par * 2048:(par + 1) * 2048]
                        xt_ap = x1tok[:, par * 2048:(par + 1) * 2048]
                        e_load = S.op("sync", lambda sy, tt=tt, st_ap=st_ap:
                                      sy.dma_start(out=st_ap,
                                                   in_=x_in[tt * 128:(tt + 1) * 128, :]),
                                      inc=("lxa0", "lxa1")[par],
                                      after=[e_apply.get(tt - 2)])
                        pre = ln_stats(st_ap, [e_load], xsq[:])
                        e_apply[tt] = S.op("vector", lambda v, st_ap=st_ap, xt_ap=xt_ap:
                                           v.tensor_scalar(xt_ap, st_ap,
                                                           stats[:, 2:3], stats[:, 6:7],
                                                           ALU.subtract, ALU.mult),
                                           inc="ve",
                                           after=pre + [e_tr_last.get(tt - 2)])
                        for fn in vl_pend:
                            fn()
                        del vl_pend[:]
                        for h in range(2):
                            def trs(t, xt_ap=xt_ap, h=h):
                                out = []
                                for dci in range(8):
                                    dc = h * 8 + dci
                                    out.append(t.transpose(
                                        pst[:, dci * 128:(dci + 1) * 128],
                                        xt_ap[:, dc * 128:(dc + 1) * 128], idn[:]))
                                return out
                            e_tr = S.op("tensor", trs, inc="te", n=8,
                                        after=[e_apply[tt], e_consts,
                                               e_evac.get((tt, h - 1) if h else (tt - 1, 1))])
                            def evac(s, tt=tt, h=h):
                                src = pst[:, 0:1024].rearrange(
                                    "p (dc t) -> p dc t", dc=8)
                                dst = x1f[:].rearrange(
                                    "p (dc t) -> p dc t", dc=16)[
                                        :, h * 8:(h + 1) * 8, tt * 128:(tt + 1) * 128]
                                return s.copy(dst, src)
                            e_evac[(tt, h)] = S.op("scalar", evac, inc="se",
                                                   after=[e_tr])
                            e_tr_last[tt] = e_tr
                        if tt >= 5:
                            emit_vl(1, tt - 5, e_evac[(tt - 1, 1)], defer=True)
                        if tt >= 4:
                            emit_vl(0, tt - 4, e_evac[(tt, 1)], defer=True)
                    e_x1f = e_evac[(15, 1)]
                    for fn in vl_pend:
                        fn()
                    del vl_pend[:]
                    emit_vl(1, 11, e_x1f)
                    emit_wv_load(2)
                    emit_wv_load(3)
                    for vc in (2, 3):
                        for tt in range(12):
                            emit_vl(vc, tt, e_x1f)
                        if vc == 2:
                            emit_wv_load(4)
                    e_wvg = S.op("gpsimd", lambda g: g.dma_start(
                        out=wvbuf[:, 6144:6144 + 2048].rearrange(
                            "p (dc n) -> p dc n", dc=16),
                        in_=wvg_img), inc="lwv1", after=list(wv_users[1]))
                    for tt in range(12):
                        emit_vl(4, tt, e_x1f)
                    for tt in range(16):
                        pid = vstate["n"] % 3
                        vstate["n"] += 1
                        psu = (ps2, ps3, ps4)[pid]
                        def mmg(t, tt=tt, psu=psu):
                            for dc in range(16):
                                ins = t.matmul(psu[:, 0:128],
                                               x1f_c(dc, tt * 128, 128),
                                               wvbuf[:, 6144 + dc * 128:
                                                     6144 + (dc + 1) * 128],
                                               start=(dc == 0), stop=(dc == 15))
                            return ins
                        e_mm = S.op("tensor", mmg, inc="te",
                                    after=[e_wvg, e_x1f, psu_last.get(pid)])
                        def evacg(v, tt=tt, psu=psu):
                            src = psu[:, 0:128].rearrange("p (h d) -> p h d", d=64)
                            dst = v65g[:, tt * 130:(tt + 1) * 130].rearrange(
                                "p (h o) -> p h o", o=65)[:, :, 0:64]
                            return v.tensor_copy(dst, src)
                        ev = S.op("vector", evacg, inc="ve", after=[e_mm])
                        psu_last[pid] = ev
                        vstate["last"] = ev
                    psu_last[0] = psu_last[1] = psu_last[2] = vstate["last"]
                    fence = S.emit(blk, sems)
                e_x1f_all = [e_evac[(tt, h)] for tt in range(16) for h in range(2)]
                y_sb = ys_stack.enter_context(
                    nc.sbuf_tensor("y_sb", [128, 16 * 1024], BF16, side="right"))

                e_taps = []
                if "t_x1f" in taps:
                    with nc.Block() as blk:
                        e_taps.append(S.op("gpsimd", lambda g: g.dma_start(
                            out=taps["t_x1f"], in_=x1f[:]), inc="ltap",
                            after=[e_x1f]))
                        S.emit(blk, sems)

                with contextlib.ExitStack() as ctxD:
                    ED = ctxD.enter_context
                    wqb = ED(nc.sbuf_tensor("wqb", [128, 2 * 2048], BF16))
                    wkb = ED(nc.sbuf_tensor("wkb", [128, 2 * 2048], BF16))
                    qraw = ED(nc.sbuf_tensor("qraw", [128, 1024], BF16))
                    kraw = ED(nc.sbuf_tensor("kraw", [128, 2048], BF16))
                    sinscr = ED(nc.sbuf_tensor("sinscr", [128, 2048], BF16))
                    qrot = ED(nc.sbuf_tensor("qrot", [128, 2 * 1024], BF16))
                    krot = ED(nc.sbuf_tensor("krot", [128, 2 * 2048], BF16))
                    psb = ED(nc.sbuf_tensor("psb", [128, 2048], BF16))
                    rbc = ED(nc.sbuf_tensor("rbc", [128, 2048], BF16))
                    onesd = ED(nc.sbuf_tensor("onesd", [1, 128], BF16))
                    ps5 = ED(nc.psum_tensor("ps5", [128, 512], F32))
                    blk = ED(nc.Block())
                    e_onesd = S.op("vector", lambda v: v.memset(
                        onesd[:], 1.0), inc="ve")
                    e_exp_par = {}
                    e_pv_lastk = {}
                    r45 = {0: psu_last.get(2), 1: None}
                    tail_state = {"pend": None, "slotg": {}, "bcg": {},
                                  "slot": 0, "ymuls": []}

                    def emit_tail():
                        st = tail_state["pend"]
                        if st is None:
                            return
                        tail_state["pend"] = None
                        sl, cols, e_rA, e_rB, e_yA, e_yB = st
                        e_bA = S.op("tensor", lambda t: t.matmul(
                            ps4[:], onesd[:],
                            rab216[0:1, sl * 1024:sl * 1024 + 512],
                            start=True, stop=True), inc="te",
                            after=[e_rA, r45[0], e_onesd])
                        e_bB = S.op("tensor", lambda t: t.matmul(
                            ps5[:], onesd[:],
                            rab216[0:1, sl * 1024 + 512:(sl + 1) * 1024],
                            start=True, stop=True), inc="te",
                            after=[e_rB, r45[1], e_onesd])
                        tail_state["bcg"][sl] = [e_bA, e_bB]
                        prevm = tail_state["slotg"].get(sl, [None, None])
                        e_eA = S.op("scalar", lambda s: s.copy(
                            rbc[:, sl * 1024:sl * 1024 + 512], ps4[:]),
                            inc="se", after=[e_bA, prevm[0]])
                        e_eB = S.op("scalar", lambda s: s.copy(
                            rbc[:, sl * 1024 + 512:(sl + 1) * 1024], ps5[:]),
                            inc="se", after=[e_bB, prevm[1]])
                        r45[0] = e_eA
                        r45[1] = e_eB
                        e_mA = S.op("vector", lambda v: v.tensor_tensor(
                            y_sb[0:64, cols[0]:cols[1]],
                            y_sb[0:64, cols[0]:cols[1]],
                            rbc[0:64, sl * 1024:sl * 1024 + 512], ALU.mult),
                            inc="ve", after=[e_eA, e_yA])
                        e_mB = S.op("vector", lambda v: v.tensor_tensor(
                            y_sb[64:128, cols[0]:cols[1]],
                            y_sb[64:128, cols[0]:cols[1]],
                            rbc[64:128, sl * 1024 + 512:(sl + 1) * 1024],
                            ALU.mult), inc="ve", after=[e_eB, e_yB])
                        tail_state["slotg"][sl] = [e_mA, e_mB]
                        tail_state["ymuls"] += [e_mA, e_mB]
                        pair_ymuls[cols[0] // 1024] = e_mB
                    wload_gate = {0: [], 1: []}
                    qk_last_pair = {}
                    e_prep = {}
                    e_attn_prev = None

                    def prep_pair(p):
                        is_g = (p == 15)
                        kcols = 2048 if is_g else 1536
                        kbase = 0 if is_g else 512
                        pp = p % 2
                        e_wq = S.op("gpsimd", lambda g, p=p, pp=pp: g.dma_start(
                            out=wqb[:, pp * 2048:(pp + 1) * 2048],
                            in_=wq_img[p]), inc=("lwq", "lwq1")[pp],
                            after=wload_gate[pp])
                        e_wk = S.op("gpsimd", lambda g, p=p, pp=pp: g.dma_start(
                            out=wkb[:, pp * 2048:(pp + 1) * 2048],
                            in_=wk_img[p]), inc=("lwk", "lwk1")[pp],
                            after=wload_gate[pp])
                        wload_gate[pp] = []
                        e_ev = {}
                        rot_par = [0]
                        def mm45(fn, pre):
                            pid = rot_par[0]
                            rot_par[0] ^= 1
                            psu = (ps4, ps5)[pid]
                            e = S.op("tensor", lambda t, fn=fn, psu=psu:
                                     fn(t, psu), inc="te",
                                     after=pre + [r45[pid]])
                            return e, pid, psu
                        for which in ("q", "k"):
                            buf, raw = (wqb, qraw) if which == "q" else (wkb, kraw)
                            base = 1024 if which == "q" else kbase
                            ntc = 2 if which == "q" else kcols // 512
                            e_w = e_wq if which == "q" else e_wk
                            for tc in range(ntc):
                                def projf(t, psu, buf=buf, tc=tc, base=base,
                                          pp=pp):
                                    for dc in range(16):
                                        ins = t.matmul(
                                            psu[:],
                                            buf[:, pp * 2048 + dc * 128:
                                                pp * 2048 + (dc + 1) * 128],
                                            x1f_c(dc, base + tc * 512, 512),
                                            start=(dc == 0), stop=(dc == 15))
                                    return ins
                                e_mm, pid, psu = mm45(projf, [e_w, e_x1f])
                                wload_gate[pp].append(e_mm)
                                e_e = S.op("vector",
                                           lambda v, raw=raw, tc=tc, psu=psu:
                                           v.tensor_copy(
                                               raw[:, tc * 512:(tc + 1) * 512],
                                               psu[:]),
                                           inc="ve", after=[e_mm])
                                r45[pid] = e_e
                                e_ev[(which, tc)] = e_e

                        def rope(which, raw, rotb, rot0, ncols, tab0):
                            e_last = None
                            for tc in range(ncols // 512):
                                t0 = tc * 512
                                def swf(t, psu, raw=raw, t0=t0):
                                    return t.matmul(psu[:, 0:512], pswap[:],
                                                    raw[:, t0:t0 + 512],
                                                    start=True, stop=True)
                                e_sw, pid, psu = mm45(
                                    swf, [e_ev[(which, tc)], e_consts,
                                          qk_last_pair.get(pp)])
                                e_cm = S.op("vector", lambda v, raw=raw, t0=t0,
                                            tab0=tab0, rot0=rot0:
                                            v.tensor_tensor(
                                    rotb[:, rot0 + t0:rot0 + t0 + 512],
                                    raw[:, t0:t0 + 512],
                                    cosT[:, tab0 + t0:tab0 + t0 + 512],
                                    ALU.mult), inc="ve",
                                    after=[e_ev[(which, tc)], e_consts,
                                           qk_last_pair.get(pp)])
                                e_sm = S.op("vector", lambda v, t0=t0,
                                            tab0=tab0, psu=psu:
                                            v.tensor_tensor(
                                    sinscr[:, t0:t0 + 512], psu[:, 0:512],
                                    sinT[:, tab0 + t0:tab0 + t0 + 512],
                                    ALU.mult), inc="ve", after=[e_sw])
                                r45[pid] = e_sm
                                e_last = S.op("vector", lambda v, t0=t0,
                                              rot0=rot0: v.tensor_tensor(
                                    rotb[:, rot0 + t0:rot0 + t0 + 512],
                                    rotb[:, rot0 + t0:rot0 + t0 + 512],
                                    sinscr[:, t0:t0 + 512], ALU.add),
                                    inc="ve", after=[e_cm, e_sm])
                            return e_last
                        e_rq = rope("q", qraw, qrot, pp * 1024, 1024, 1024)
                        e_rk = rope("k", kraw, krot, pp * 2048, kcols, kbase)
                        e_prep[p] = (e_rq, e_rk)

                    prep_pair(0)
                    for p in range(16):
                        if p + 1 < 16:
                            prep_pair(p + 1)
                        is_g = (p == 15)
                        nktl, dsl = (GLB_NKT, GLB_DS) if is_g else (LOC_NKT, LOC_DS)
                        vt, vstride = (v65g, 130) if is_g else (v65l, 1950)
                        pp = p % 2
                        e_rq, e_rk = e_prep.pop(p)
                        e_qk_events = []

                        for qc in range(2):
                            nkt, ds = nktl[qc], dsl[qc]
                            e_pv = {}
                            pend = None
                            for kt in range(nkt + 1):
                                if kt == 2:
                                    emit_tail()
                                if kt < nkt:
                                    j0 = max(0, kt - ds)
                                    qo = j0 * 128
                                    par = kt % 2
                                    psu = (ps0, ps1)[par]
                                    slot = (24 + qc * 16 + kt) if is_g else (qc * 12 + kt)

                                    def qk(t, kt=kt, qc=qc, qo=qo, psu=psu,
                                           pp=pp):
                                        a = t.matmul(
                                            psu[0:128, qo:512],
                                            krot[0:64, pp * 2048 + kt * 128:
                                                 pp * 2048 + (kt + 1) * 128],
                                            qrot[0:64, pp * 1024 + qc * 512 + qo:
                                                 pp * 1024 + (qc + 1) * 512],
                                            start=True, stop=True)
                                        b = t.matmul(
                                            psu[0:128, 512 + qo:1024],
                                            krot[64:128, pp * 2048 + kt * 128:
                                                 pp * 2048 + (kt + 1) * 128],
                                            qrot[64:128, pp * 1024 + qc * 512 + qo:
                                                 pp * 1024 + (qc + 1) * 512],
                                            start=True, stop=True)
                                        return [a, b]
                                    e_qk = S.op("tensor", qk, inc="te", n=2,
                                                after=[e_rq, e_rk,
                                                       e_exp_par.get(par)])
                                    e_qk_events.append(e_qk)
                                if pend is not None:
                                    pkt, pgate = pend
                                    pqo = max(0, pkt - ds) * 128
                                    ppar = pkt % 2
                                    for hh in range(2):
                                        pvp = (ps2, ps3)[hh]
                                        vcol0 = (hh if is_g else 2 * p + hh) * 65
                                        def pv(t, hh=hh, pkt=pkt, pqo=pqo,
                                               ppar=ppar, pvp=pvp, vt=vt,
                                               vstride=vstride, vcol0=vcol0,
                                               nkt=nkt):
                                            return t.matmul(
                                                pvp[0:65, pqo:512],
                                                vt[:, pkt * vstride + vcol0:
                                                   pkt * vstride + vcol0 + 65],
                                                psb[:, ppar * 1024 + hh * 512 + pqo:
                                                    ppar * 1024 + hh * 512 + 512],
                                                start=(pkt == 0),
                                                stop=(pkt == nkt - 1))
                                        pre_pv = [pgate]
                                        if pkt == 0:
                                            pre_pv += [psu_last.get(hh),
                                                       e_ones_l, e_ones_g]
                                        e_pv[(pkt, hh)] = S.op(
                                            "tensor", pv, inc="te",
                                            after=pre_pv)
                                    e_pv_lastk[ppar] = e_pv[(pkt, 1)]
                                    pend = None
                                if kt < nkt:
                                    def expf(s, par=par, qo=qo, psu=psu,
                                             slot=slot):
                                        a = s.activation(
                                            psb[:, par * 1024 + qo:
                                                par * 1024 + 512],
                                            psu[0:128, qo:512], AF.Exp,
                                            bias=biasv[:, slot:slot + 1],
                                            scale=0.125)
                                        b = s.activation(
                                            psb[:, par * 1024 + 512 + qo:
                                                par * 1024 + 1024],
                                            psu[0:128, 512 + qo:1024], AF.Exp,
                                            bias=biasv[:, slot:slot + 1],
                                            scale=0.125)
                                        return [a, b]
                                    e_ex = S.op("scalar", expf, inc="se", n=2,
                                                after=[e_qk,
                                                       e_pv_lastk.get(par),
                                                       e_consts])
                                    e_exp_par[par] = e_ex
                                    e_gate = e_ex
                                    if kt >= ds:
                                        def trif(v, par=par, qo=qo):
                                            a = v.tensor_tensor(
                                                psb[:, par * 1024 + qo:
                                                    par * 1024 + qo + 128],
                                                psb[:, par * 1024 + qo:
                                                    par * 1024 + qo + 128],
                                                tri[:], ALU.mult)
                                            b = v.tensor_tensor(
                                                psb[:, par * 1024 + 512 + qo:
                                                    par * 1024 + 512 + qo + 128],
                                                psb[:, par * 1024 + 512 + qo:
                                                    par * 1024 + 512 + qo + 128],
                                                tri[:], ALU.mult)
                                            return [a, b]
                                        e_gate = S.op("vector", trif, inc="ve",
                                                      n=2, after=[e_ex])
                                    pend = (kt, e_gate)

                            sl = tail_state["slot"]
                            tail_state["slot"] ^= 1
                            cols = (p * 1024 + qc * 512,
                                    p * 1024 + (qc + 1) * 512)
                            pg = tail_state["slotg"].get(sl, [None, None])
                            bcg = tail_state["bcg"].get(sl, [None, None])
                            e_yA = S.op("scalar", lambda s, cols=cols:
                                        s.copy(y_sb[0:64, cols[0]:cols[1]],
                                               ps2[0:64, :]), inc="se",
                                        after=[e_pv[(nkt - 1, 0)], pg[0]])
                            e_yB = S.op("scalar", lambda s, cols=cols:
                                        s.copy(y_sb[64:128, cols[0]:cols[1]],
                                               ps3[0:64, :]), inc="se",
                                        after=[e_pv[(nkt - 1, 1)], pg[1]])
                            def _recip(v, dst0, src):
                                with nc.allow_low_precision(
                                        reason="softmax 1/denom as bf16; "
                                        "bf16-matmul-grade factor"):
                                    return v.reciprocal(
                                        rab216[0:1, dst0:dst0 + 512], src)
                            e_rA = S.op("vector", lambda v, sl=sl:
                                        _recip(v, sl * 1024, ps2[64:65, :]),
                                        inc="ve",
                                        after=[e_pv[(nkt - 1, 0)], bcg[0]])
                            e_rB = S.op("vector", lambda v, sl=sl:
                                        _recip(v, sl * 1024 + 512,
                                               ps3[64:65, :]), inc="ve",
                                        after=[e_pv[(nkt - 1, 1)], bcg[1]])
                            psu_last[0] = [e_yA, e_rA]
                            psu_last[1] = [e_yB, e_rB]
                            tail_state["pend"] = (sl, cols, e_rA, e_rB,
                                                  e_yA, e_yB)
                        qk_last_pair[pp] = e_qk_events[-1]
                    emit_tail()
                    e_y_done = tail_state["ymuls"][-1]
                    psu_last[0] = tail_state["ymuls"][-2]
                    psu_last[1] = tail_state["ymuls"][-1]
                    psu_last[2] = r45[0]
                    for _pid in (3, 4, 5, 6):
                        psu_last[_pid] = [e_exp_par[0], e_exp_par[1], r45[1]]
                    fence = S.emit(blk, sems)

            if "t_y" in taps:
                with nc.Block() as blk:
                    e_taps.append(S.op("gpsimd", lambda g: g.dma_start(
                        out=taps["t_y"], in_=y_sb[:]), inc="ltap",
                        after=[e_y_done]))
                    S.emit(blk, sems)

            with (
                nc.sbuf_tensor("x2", [128, 2 * 2048], F32) as x2,
                nc.sbuf_tensor("x3f", [128, 16 * 1024], BF16) as x3f,
                nc.sbuf_tensor("x3tok", [128, 2 * 2048], BF16) as x3tok,
                nc.psum_tensor("pst2", [128, 1024], BF16) as pst,
            ):
                e_x3evac = {}
                x3state = {"prev_tr": None, "last_tr3": None}
                with contextlib.ExitStack() as ctxO:
                    EO = ctxO.enter_context
                    wob = EO(nc.sbuf_tensor("wob", [128, 4 * 8192], BF16))
                    bo_sb = EO(nc.sbuf_tensor("bo_sb", [1, 2048], BF16))
                    onecol = EO(nc.sbuf_tensor("onecol", [1, 128], BF16))
                    stage2 = EO(nc.sbuf_tensor("stage2", [128, 2 * 2048], F32))
                    xsq2 = EO(nc.sbuf_tensor("xsq2", [128, 512], F32))
                    st2s = EO(nc.sbuf_tensor("st2s", [128, 16], F32))
                    sacc = EO(nc.sbuf_tensor("sacc", [128, 4], F32))
                    blk = EO(nc.Block())
                    e_bo = S.op("gpsimd", lambda g: g.dma_start(
                        out=bo_sb[:], in_=bo_in), inc="lbo", after=e_taps)
                    e_one = S.op("vector", lambda v: v.memset(onecol[:], 1.0),
                                 inc="ve")
                    e_wo_d = {}
                    for dcn in range(4):
                        e_wo_d[dcn] = S.op("gpsimd", lambda g, dcn=dcn:
                                           g.dma_start(
                            out=wob[:, dcn * 8192:(dcn + 1) * 8192].rearrange(
                                "p (fc n) -> p fc n", fc=16),
                            in_=wo_img[dcn]),
                            inc=("lwo", "lwo1", "lwv", "lwv1")[dcn],
                            after=(e_taps if dcn == 0 else []))
                    sred_last = {}
                    ssq_last = {}
                    e_x2done = {}
                    x2_free = {}
                    x3tr = {0: None, 1: None}
                    prev_tr = {"tr": None}
                    opsus = [ps2[:], ps3[:], ps4[:], ps0[:, 0:512],
                             ps0[:, 512:1024], ps1[:, 0:512],
                             ps1[:, 512:1024]]
                    tr3_pend = []
                    for tt in range(8):
                        par = tt % 2
                        sb = 8 * par
                        x2t = x2[:, par * 2048:(par + 1) * 2048]
                        x3t = x3tok[:, par * 2048:(par + 1) * 2048]
                        st2 = stage2[:, par * 2048:(par + 1) * 2048]
                        free_pre = x2_free.get(tt - 2, [])
                        sacm = {}
                        e_ld2 = S.op("sync", lambda sy, tt=tt, st2=st2:
                                     sy.dma_start(out=st2,
                                                  in_=x_in[1024 + tt * 128:
                                                           1024 + (tt + 1) * 128, :]),
                                     inc=("lo0", "lo1")[par],
                                     after=[e_x2done.get(tt - 2)] +
                                     (e_taps if tt < 2 else []))
                        for dcn in range(4):
                            pid = (tt * 4 + dcn) % 7
                            psu = opsus[pid]
                            for fc in range(16):
                                def mmo1(t, fc=fc, tt=tt, psu=psu, dcn=dcn):
                                    return t.matmul(
                                        psu,
                                        y_sb[:, fc * 1024 + tt * 128:
                                             fc * 1024 + (tt + 1) * 128],
                                        wob[:, dcn * 8192 + fc * 512:
                                            dcn * 8192 + (fc + 1) * 512],
                                        start=(fc == 0), stop=False)
                                e_mm = S.op("tensor", mmo1, inc="te",
                                            after=[e_wo_d[dcn],
                                                   pair_ymuls.get(fc),
                                                   (psu_last.get(pid)
                                                    if fc == 0 else None)] +
                                            (e_taps if fc == 0 else []))
                            def mmob(t, dcn=dcn, psu=psu):
                                return t.matmul(psu, onecol[:],
                                                bo_sb[:, dcn * 512:(dcn + 1) * 512],
                                                start=False, stop=True)
                            e_mm = S.op("tensor", mmob, inc="te",
                                        after=[e_bo, e_one])
                            ev = S.op("scalar",
                                      lambda s, dcn=dcn, x2t=x2t, psu=psu:
                                      s.copy(x2t[:, dcn * 512:(dcn + 1) * 512],
                                             psu[:]), inc="se",
                                      after=[e_mm] + free_pre)
                            psu_last[pid] = ev
                            sl = (tt * 4 + dcn) % 2
                            x2c = x2t[:, dcn * 512:(dcn + 1) * 512]
                            e_red = S.op("vector", lambda v, x2c=x2c, sl=sl:
                                         v.tensor_reduce(
                                sacc[:, sl:sl + 1], x2c, AX, ALU.add),
                                inc="ve", after=[ev, sred_last.get(sl)])
                            if dcn == 0:
                                e_a1 = S.op("vector", lambda v, tt=tt, sl=sl:
                                            v.tensor_copy(
                                    st2s[:, 2 * tt:2 * tt + 1],
                                    sacc[:, sl:sl + 1]), inc="ve",
                                    after=[e_red])
                            else:
                                e_a1 = S.op("vector", lambda v, tt=tt, sl=sl:
                                            v.tensor_tensor(
                                    st2s[:, 2 * tt:2 * tt + 1],
                                    st2s[:, 2 * tt:2 * tt + 1],
                                    sacc[:, sl:sl + 1], ALU.add), inc="ve",
                                    after=[e_red, sacm[(tt, 0)]])
                            sred_last[sl] = e_a1
                            sacm[(tt, 0)] = e_a1
                            e_sq = S.op("scalar", lambda s, x2c=x2c, sl=sl:
                                        s.activation(
                                xsq2[:, 0:512], x2c, AF.Square,
                                accum_out=sacc[:, 2 + sl:3 + sl]), inc="se",
                                after=[ev, ssq_last.get(sl)])
                            if dcn == 0:
                                e_a2 = S.op("vector", lambda v, tt=tt, sl=sl:
                                            v.tensor_copy(
                                    st2s[:, 2 * tt + 1:2 * tt + 2],
                                    sacc[:, 2 + sl:3 + sl]), inc="ve",
                                    after=[e_sq])
                            else:
                                e_a2 = S.op("vector", lambda v, tt=tt, sl=sl:
                                            v.tensor_tensor(
                                    st2s[:, 2 * tt + 1:2 * tt + 2],
                                    st2s[:, 2 * tt + 1:2 * tt + 2],
                                    sacc[:, 2 + sl:3 + sl], ALU.add),
                                    inc="ve",
                                    after=[e_sq, sacm[(tt, 1)]])
                            ssq_last[sl] = e_a2
                            sacm[(tt, 1)] = e_a2
                        if len(tr3_pend) >= 2:
                            tr3_pend.pop(0)()
                        pre_p = [sacm[(tt, 0)], sacm[(tt, 1)]]
                        c1 = S.op("vector", lambda v, tt=tt, sb=sb:
                                  v.tensor_scalar(
                            stats[:, sb + 2:sb + 3], st2s[:, 2 * tt:2 * tt + 1],
                            1.0 / D, None, ALU.mult), inc="ve", after=pre_p)
                        c2 = S.op("vector", lambda v, sb=sb: v.tensor_tensor(
                            stats[:, sb + 4:sb + 5], stats[:, sb + 2:sb + 3],
                            stats[:, sb + 2:sb + 3],
                            ALU.mult), inc="ve", after=[c1])
                        c3 = S.op("vector", lambda v, tt=tt, sb=sb:
                                  v.tensor_scalar(
                            stats[:, sb + 3:sb + 4],
                            st2s[:, 2 * tt + 1:2 * tt + 2],
                            1.0 / D, None, ALU.mult), inc="ve", after=pre_p)
                        c4 = S.op("vector", lambda v, sb=sb: v.tensor_tensor(
                            stats[:, sb + 3:sb + 4], stats[:, sb + 3:sb + 4],
                            stats[:, sb + 4:sb + 5],
                            ALU.subtract), inc="ve", after=[c2, c3])
                        c5 = S.op("vector", lambda v, sb=sb: v.tensor_scalar(
                            stats[:, sb + 3:sb + 4], stats[:, sb + 3:sb + 4],
                            EPS, None, ALU.add), inc="ve", after=[c4])
                        e_ch = S.op("vector", lambda v, sb=sb: v.reciprocal(
                            stats[:, sb + 5:sb + 6], stats[:, sb + 3:sb + 4]),
                            inc="ve", after=[c5])
                        e_rstd = S.op("scalar", lambda s, sb=sb: s.activation(
                            stats[:, sb + 6:sb + 7], stats[:, sb + 5:sb + 6],
                            AF.Sqrt), inc="se", after=[e_ch])
                        pre = [c1, e_ch, e_rstd]
                        e_ap2 = S.op("vector", lambda v, x2t=x2t, sb=sb:
                                     v.tensor_scalar(x2t, x2t,
                                                     stats[:, sb + 2:sb + 3],
                                                     stats[:, sb + 6:sb + 7],
                                                     ALU.subtract, ALU.mult),
                                     inc="ve", after=pre)
                        e_res = S.op("vector", lambda v, x2t=x2t, st2=st2:
                                     v.tensor_tensor(x2t, x2t, st2, ALU.add),
                                     inc="ve", after=[e_ap2, e_ld2])
                        e_x2done[tt] = e_res
                        e_st = S.op("sync", lambda sy, tt=tt, x2t=x2t:
                                    sy.dma_start(
                            out=out_x2[tt * 128:(tt + 1) * 128, :], in_=x2t),
                            inc=("st", "st1")[par], after=[e_res])
                        e3_sum = S.op("vector", lambda v, x2t=x2t, sb=sb:
                                      v.tensor_reduce(stats[:, sb:sb + 1], x2t,
                                                      AX, ALU.add), inc="ve",
                                      after=[e_res])
                        e3_sq = S.op("scalar", lambda sc, x2t=x2t, x3t=x3t,
                                     sb=sb: sc.activation(
                            x3t, x2t, AF.Square,
                            accum_out=stats[:, sb + 1:sb + 2]), inc="se",
                            after=[e_res, x3tr[par]])
                        c31 = S.op("vector", lambda v, sb=sb: v.tensor_scalar(
                            stats[:, sb + 2:sb + 3], stats[:, sb:sb + 1],
                            1.0 / D, None,
                            ALU.mult), inc="ve", after=[e3_sum])
                        c32 = S.op("vector", lambda v, sb=sb: v.tensor_tensor(
                            stats[:, sb + 4:sb + 5], stats[:, sb + 2:sb + 3],
                            stats[:, sb + 2:sb + 3],
                            ALU.mult), inc="ve", after=[c31])
                        c33 = S.op("vector", lambda v, sb=sb: v.tensor_scalar(
                            stats[:, sb + 3:sb + 4], stats[:, sb + 1:sb + 2],
                            1.0 / D, None,
                            ALU.mult), inc="ve", after=[e3_sq])
                        c34 = S.op("vector", lambda v, sb=sb: v.tensor_tensor(
                            stats[:, sb + 3:sb + 4], stats[:, sb + 3:sb + 4],
                            stats[:, sb + 4:sb + 5],
                            ALU.subtract), inc="ve", after=[c32, c33])
                        c35 = S.op("vector", lambda v, sb=sb: v.tensor_scalar(
                            stats[:, sb + 3:sb + 4], stats[:, sb + 3:sb + 4],
                            EPS, None, ALU.add),
                            inc="ve", after=[c34])
                        e3_ch = S.op("vector", lambda v, sb=sb: v.reciprocal(
                            stats[:, sb + 5:sb + 6], stats[:, sb + 3:sb + 4]),
                            inc="ve", after=[c35])
                        e3_rstd = S.op("scalar", lambda s, sb=sb: s.activation(
                            stats[:, sb + 6:sb + 7], stats[:, sb + 5:sb + 6],
                            AF.Sqrt), inc="se", after=[e3_ch])
                        pre3 = [c31, e3_ch, e3_rstd]
                        e_ap3 = S.op("vector", lambda v, x2t=x2t, x3t=x3t,
                                     sb=sb: v.tensor_scalar(
                                         x3t, x2t,
                                         stats[:, sb + 2:sb + 3],
                                         stats[:, sb + 6:sb + 7],
                                         ALU.subtract, ALU.mult),
                                     inc="ve",
                                     after=pre3 + [x3tr[par]])

                        def emit_tr3(tt=tt, par=par, x3t=x3t, e_ap3=e_ap3):
                            for h in range(2):
                                def trs(t, h=h, x3t=x3t):
                                    out = []
                                    for dci in range(8):
                                        dc = h * 8 + dci
                                        out.append(t.transpose(
                                            pst[:, dci * 128:(dci + 1) * 128],
                                            x3t[:, dc * 128:(dc + 1) * 128],
                                            idn[:]))
                                    return out
                                e_tr = S.op("tensor", trs, inc="te", n=8,
                                            after=[e_ap3, prev_tr["tr"]])
                                def evac3(vv, tt=tt, h=h):
                                    src = pst[:, 0:1024].rearrange(
                                        "p (dc t) -> p dc t", dc=8)
                                    dst = x3f[:].rearrange(
                                        "p (dc t) -> p dc t", dc=16)[
                                            :, h * 8:(h + 1) * 8,
                                            tt * 128:(tt + 1) * 128]
                                    return vv.tensor_copy(dst, src)
                                ev3 = S.op("vector", evac3, inc="ve",
                                           after=[e_tr])
                                e_x3evac[(tt, h)] = ev3
                                prev_tr["tr"] = ev3
                                x3tr[par] = e_tr
                        tr3_pend.append(emit_tr3)
                        x2_free[tt] = [e_ap3, e3_sq, e_st]
                    for fn in tr3_pend:
                        fn()
                    del tr3_pend[:]
                    fence = S.emit(blk, sems)
                ys_stack.close()

                with contextlib.ExitStack() as ctxF:
                    EF = ctxF.enter_context
                    h1 = EF(nc.sbuf_tensor("h1", [128, 16 * 1024], BF16))
                    w1b = EF(nc.sbuf_tensor("w1b", [128, 2 * 2048], BF16))
                    g2c = EF(nc.sbuf_tensor("g2c", [128, 8 * 512], BF16))
                    wfb = EF(nc.sbuf_tensor("wfb", [128, 2 * 8192], BF16))
                    wg = EF(nc.sbuf_tensor("wg", [128, 64], BF16))
                    wrb = EF(nc.sbuf_tensor("wrb", [128, 128], BF16))
                    w2b = EF(nc.sbuf_tensor("w2b", [128, 4 * 2048], BF16))
                    w2f = EF(nc.sbuf_tensor("w2f", [128, 2 * 2048], F32))
                    g2b = EF(nc.sbuf_tensor("g2b", [128, 128], BF16))
                    psout = EF(nc.sbuf_tensor("psout", [8, 2048], F32))
                    rlog = EF(nc.sbuf_tensor("rlog", [128, 16], F32))
                    rt = EF(nc.sbuf_tensor("rt", [128, 48], F32))
                    blk = EF(nc.Block())
                    e_x3f = [e_x3evac[(tt, h)] for tt in range(8) for h in range(2)]

                    if "t_x3f" in taps:
                        S.op("gpsimd", lambda g: g.dma_start(
                            out=taps["t_x3f"], in_=x3f[:]), inc="ltap",
                            after=e_x3f)

                    e_wr = S.op("gpsimd", lambda g: g.dma_start(
                        out=wrb[:].rearrange("p (dc n) -> p dc n", dc=16),
                        in_=wr_img), inc="lwr")
                    e_wg_done = []
                    rp = {}
                    for tt in range(8):
                        rsl = (tt % 2) * 8
                        ro = (tt % 2) * 24
                        rg = rlog[:, rsl:rsl + 8]
                        def mmr(t, tt=tt, rsl=rsl):
                            for dc in range(16):
                                ins = t.matmul(ps4[0:128, rsl:rsl + 8],
                                               x3f[:, dc * 1024 + tt * 128:
                                                   dc * 1024 + (tt + 1) * 128],
                                               wrb[:, dc * 8:(dc + 1) * 8],
                                               start=(dc == 0), stop=(dc == 15))
                            return ins
                        e_mr = S.op("tensor", mmr, inc="te",
                                    after=[e_wr,
                                           rp.get(tt % 2, psu_last.get(2)),
                                           e_x3evac[(tt, 0)], e_x3evac[(tt, 1)]])

                        e_g0 = S.op("vector", lambda v, rsl=rsl, ro=ro:
                                    v.tensor_reduce(
                            rt[:, ro:ro + 1], ps4[0:128, rsl:rsl + 8],
                            AX, ALU.max),
                            inc="ve", after=[e_mr])
                        e_g1 = S.op("vector", lambda v, ro=ro: v.tensor_scalar(
                            rt[:, ro + 1:ro + 2], rt[:, ro:ro + 1],
                            -1.0, None, ALU.mult),
                            inc="ve", after=[e_g0])
                        e_g2 = S.op("scalar", lambda s, rsl=rsl, ro=ro, rg=rg:
                                    s.activation(
                            rg, ps4[0:128, rsl:rsl + 8], AF.Exp,
                            bias=rt[:, ro + 1:ro + 2]), inc="se", after=[e_g1])
                        rp[tt % 2] = e_g2
                        psu_last[2] = e_g2

                        q1 = S.op("vector", lambda v, ro=ro, rg=rg:
                                  v.tensor_scalar(
                            rt[:, ro + 2:ro + 10], rg, 1.0, None, ALU.is_ge),
                            inc="ve", after=[e_g2])
                        q2 = S.op("vector", lambda v, ro=ro, rg=rg:
                                  v.tensor_tensor(
                            rt[:, ro + 2:ro + 10], rg, rt[:, ro + 2:ro + 10],
                            ALU.subtract),
                            inc="ve", after=[q1])
                        q3 = S.op("vector", lambda v, ro=ro: v.tensor_reduce(
                            rt[:, ro + 10:ro + 11], rt[:, ro + 2:ro + 10],
                            AX, ALU.max),
                            inc="ve", after=[q2])
                        q4 = S.op("vector", lambda v, ro=ro, rg=rg:
                                  v.tensor_scalar(
                            rt[:, ro + 11:ro + 19], rg,
                            rt[:, ro + 10:ro + 11], None,
                            ALU.is_ge), inc="ve", after=[q3])
                        q5 = S.op("vector", lambda v, ro=ro, rg=rg:
                                  v.tensor_tensor(
                            rt[:, ro + 11:ro + 19], rg,
                            rt[:, ro + 11:ro + 19], ALU.mult),
                            inc="ve", after=[q4])
                        q6 = S.op("vector", lambda v, ro=ro: v.tensor_scalar(
                            rt[:, ro + 10:ro + 11], rt[:, ro + 10:ro + 11],
                            1.0, None, ALU.add),
                            inc="ve", after=[q4])
                        q7 = S.op("vector", lambda v, ro=ro: v.reciprocal(
                            rt[:, ro + 20:ro + 21], rt[:, ro + 10:ro + 11]),
                            inc="ve", after=[q6])
                        q8 = S.op("vector", lambda v, tt=tt, ro=ro:
                                  v.tensor_scalar(
                            wg[:, tt * 8:(tt + 1) * 8],
                            rt[:, ro + 11:ro + 19],
                            rt[:, ro + 20:ro + 21], None, ALU.mult),
                            inc="ve",
                            after=[q5, q7])
                        e_wg_done.append(q8)
                    if "t_wg" in taps:
                        S.op("gpsimd", lambda g: g.dma_start(
                            out=taps["t_wg"], in_=wg[:]), inc="ltap",
                            after=e_wg_done)

                    e_h1 = {}
                    w1_users = {0: [], 1: []}
                    e_w1_d = {}

                    def emit_w1_load(ic):
                        wp = ic % 2
                        e_w1_d[ic] = S.op("gpsimd", lambda g, ic=ic, wp=wp:
                                          g.dma_start(
                            out=w1b[:, wp * 2048:(wp + 1) * 2048],
                            in_=w1_img[ic]), inc=("lw1", "lw11")[wp],
                            after=list(w1_users[wp]))
                        w1_users[wp] = []
                    emit_w1_load(0)
                    emit_w1_load(1)
                    e_w1_users = []
                    for ic in range(16):
                        wp = ic % 2
                        e_w1 = e_w1_d[ic]
                        for tc in range(2):
                            pid = tc
                            psu = (ps2, ps3)[pid]
                            def mm1(t, ic=ic, tc=tc, psu=psu, wp=wp):
                                for dc in range(16):
                                    ins = t.matmul(
                                        psu[:],
                                        w1b[:, wp * 2048 + dc * 128:
                                            wp * 2048 + (dc + 1) * 128],
                                        x3f[:, dc * 1024 + tc * 512:
                                            dc * 1024 + (tc + 1) * 512],
                                        start=(dc == 0), stop=(dc == 15))
                                return ins
                            e_mm = S.op("tensor", mm1, inc="te",
                                        after=[e_w1, psu_last.get(pid)] +
                                        [e_x3evac[(4 * tc + i, hh)]
                                         for i in range(4) for hh in range(2)])
                            w1_users[wp].append(e_mm)
                            e_w1_users.append(e_mm)
                            ev = S.op("scalar", lambda s, ic=ic, tc=tc, psu=psu:
                                      s.activation(h1[:, ic * 1024 + tc * 512:
                                                      ic * 1024 + (tc + 1) * 512],
                                                   psu[:], AF.Silu),
                                      inc="se", after=[e_mm])
                            psu_last[pid] = ev
                            e_h1[(ic, tc)] = ev
                        if tc == 1 and ic + 2 < 16:
                            emit_w1_load(ic + 2)
                    e_h1_all = list(e_h1.values())
                    if "t_h1" in taps:
                        S.op("gpsimd", lambda g: g.dma_start(
                            out=taps["t_h1"], in_=h1[:]), inc="ltap",
                            after=e_h1_all)

                    e_agg = {}
                    w3_users = {0: [], 1: []}
                    e_w3_d = {}

                    def emit_w3_load(i2c):
                        wp = i2c % 2
                        e_w3_d[i2c] = S.op("gpsimd", lambda g, i2c=i2c, wp=wp:
                                           g.dma_start(
                            out=wfb[:, wp * 8192:(wp + 1) * 8192].rearrange(
                                "p (ic n) -> p ic n", ic=16),
                            in_=w3_img[i2c]), inc=("lw3", "lw31")[wp],
                            after=list(w3_users[wp]))
                        w3_users[wp] = []
                    emit_w3_load(0)
                    emit_w3_load(1)
                    w2_users = {0: [], 1: []}
                    e_w2_d = {}
                    e_w2_cast = {}

                    def emit_w2_load(i2s):
                        wp = i2s % 2
                        e_w2_d[i2s] = S.op("sync", lambda sy, i2s=i2s, wp=wp:
                                           sy.dma_start(
                            out=w2f[:, wp * 2048:(wp + 1) * 2048],
                            in_=w2_img[i2s]), inc=("lw2a", "lw2b")[wp],
                            after=[e_w2_cast.get(i2s - 2)])
                        w2_users[wp] = []
                    emit_w2_load(0)
                    emit_w2_load(1)
                    e_mmw_last = {}
                    e_mmw_by = {}
                    mmw_pend = []
                    for i2c in range(4):
                        wp = i2c % 2
                        e_w3 = e_w3_d[i2c]
                        e_g2c = {}
                        for tt in range(8):
                            pid = tt % 2
                            psu = (ps2, ps3)[pid]
                            def mm2(t, i2c=i2c, tt=tt, psu=psu, wp=wp):
                                for ic in range(16):
                                    ins = t.matmul(
                                        psu[:],
                                        h1[:, ic * 1024 + tt * 128:
                                           ic * 1024 + (tt + 1) * 128],
                                        wfb[:, wp * 8192 + ic * 512:
                                            wp * 8192 + (ic + 1) * 512],
                                        start=(ic == 0), stop=(ic == 15))
                                return ins
                            e_mm = S.op("tensor", mm2, inc="te",
                                        after=[e_w3, psu_last.get(pid),
                                               e_g2c.get(tt - 2)] + e_h1_all)
                            w3_users[wp].append(e_mm)
                            ev = S.op("scalar", lambda s, tt=tt, psu=psu:
                                      s.activation(g2c[:, tt * 512:(tt + 1) * 512],
                                                   psu[:], AF.Silu),
                                      inc="se", after=[e_mm])
                            psu_last[pid] = ev
                            e_g2c[tt] = ev
                        if i2c + 2 < 4:
                            emit_w3_load(i2c + 2)
                        for i2q in range(4):
                            i2s = i2c * 4 + i2q
                            last = None
                            for tt in range(8):
                                def mma(t, i2s=i2s, i2q=i2q, tt=tt):
                                    return t.matmul(
                                        ps4[0:128, i2s * 8 % 512:
                                            i2s * 8 % 512 + 8]
                                        if False else
                                        ps4[0:128, (i2s % 16) * 8:
                                            (i2s % 16) * 8 + 8],
                                        g2c[:, tt * 512 + i2q * 128:
                                            tt * 512 + (i2q + 1) * 128],
                                        wg[:, tt * 8:(tt + 1) * 8],
                                        start=(tt == 0), stop=(tt == 7))
                                pre = [e_g2c[tt]] + e_wg_done
                                if tt == 0:
                                    pre.append(psu_last.get(2))
                                last = S.op("tensor", mma, inc="te", after=pre)
                            ev = S.op("vector", lambda v, i2s=i2s: v.tensor_copy(
                                gsb[:, i2s * 8:(i2s + 1) * 8],
                                ps4[0:128, (i2s % 16) * 8:(i2s % 16) * 8 + 8]),
                                inc="ve", after=[last])
                            e_agg[i2s] = ev
                            e_cb = S.op("vector", lambda v, i2s=i2s:
                                        v.tensor_copy(
                                g2b[:, i2s * 8:(i2s + 1) * 8],
                                gsb[:, i2s * 8:(i2s + 1) * 8]),
                                inc="ve", after=[ev])
                            e_w2_cast[i2s] = S.op(
                                "vector", lambda v, i2s=i2s: v.tensor_copy(
                                    w2b[:, (i2s % 4) * 2048:
                                        (i2s % 4 + 1) * 2048],
                                    w2f[:, (i2s % 2) * 2048:
                                        (i2s % 2 + 1) * 2048]),
                                inc="ve",
                                after=[e_w2_d[i2s], e_mmw_by.get(i2s - 4)])

                            def emit_mmw(i2s=i2s, e_cb=e_cb):
                                for dcn in range(4):
                                    psu_ap = (ps0, ps0, ps1, ps1)[dcn][
                                        0:8,
                                        (dcn % 2) * 512:(dcn % 2) * 512 + 512]
                                    def mmw(t, i2s=i2s, dcn=dcn,
                                            psu_ap=psu_ap):
                                        return t.matmul(
                                            psu_ap,
                                            g2b[:, i2s * 8:(i2s + 1) * 8],
                                            w2b[:, (i2s % 4) * 2048 +
                                                dcn * 512:
                                                (i2s % 4) * 2048 +
                                                (dcn + 1) * 512],
                                            start=(i2s == 0),
                                            stop=(i2s == 15))
                                    pre_w = [e_cb, e_w2_cast[i2s]]
                                    if i2s == 0:
                                        pre_w.append(psu_last.get(3 + dcn))
                                    e_mmw_last[dcn] = S.op("tensor", mmw,
                                                           inc="te",
                                                           after=pre_w)
                                e_mmw_by[i2s] = e_mmw_last[3]
                            if mmw_pend:
                                mmw_pend.pop(0)()
                            mmw_pend.append(emit_mmw)
                            if i2s + 2 < 16:
                                emit_w2_load(i2s + 2)
                        psu_last[2] = e_agg[i2c * 4 + 3]
                    for fn in mmw_pend:
                        fn()
                    del mmw_pend[:]
                    e_po0 = S.op("vector", lambda v: v.tensor_copy(
                        psout[0:8, 0:1024], ps0[0:8, 0:1024]), inc="ve",
                        after=[e_mmw_last[0], e_mmw_last[1]])
                    e_po1 = S.op("vector", lambda v: v.tensor_copy(
                        psout[0:8, 1024:2048], ps1[0:8, 0:1024]), inc="ve",
                        after=[e_mmw_last[2], e_mmw_last[3]])
                    S.op("sync", lambda sy: sy.dma_start(
                        out=out_ps, in_=psout[:]), inc="st",
                        after=[e_po0, e_po1])
                    S.op("sync", lambda sy: sy.dma_start(
                        out=out_g, in_=gsb[:]), inc="st",
                        after=list(e_agg.values()))
                    st_total = S.counts.get("st", 0)
                    st1_total = S.counts.get("st1", 0)
                    ltap_total = S.counts.get("ltap", 0)
                    S.op("sync", lambda sy: sy.wait_ge(sems["st"], st_total))
                    if st1_total:
                        S.op("sync", lambda sy: sy.wait_ge(sems["st1"],
                                                           st1_total))
                    if ltap_total:
                        S.op("gpsimd", lambda g: g.wait_ge(sems["ltap"],
                                                           ltap_total))
                    fence = S.emit(blk, sems)

    return nc, taps


def build_fix():
    nc = bass.Bass("TRN2", target_bir_lowering=False)
    ap = lambda *a, **kw: nc.dram_tensor(*a, **kw).ap()
    ps_in = ap("ps_in", [128, 2048], F32, kind="ExternalInput")
    sel_in = ap("sel_in", [128, 8], F32, kind="ExternalInput")
    x2r_in = ap("x2r_in", [2, 8, 2048], F32, kind="ExternalInput")
    rows_out = ap("rows_out", [8, 2048], F32, kind="ExternalOutput")

    with contextlib.ExitStack() as ctx:
        E = ctx.enter_context
        sems = {k: E(nc.semaphore(k)) for k in
                ("te", "ve", "se", "lc", "lxa0", "st")}
        S = Sched()
        pf = E(nc.sbuf_tensor("pf", [128, 2048], F32))
        rf = E(nc.sbuf_tensor("rf", [128, 2048], F32))
        pb = E(nc.sbuf_tensor("pb", [128, 2048], BF16))
        rb = E(nc.sbuf_tensor("rb", [128, 2048], BF16))
        selb = E(nc.sbuf_tensor("selb", [128, 8], BF16))
        rows = E(nc.sbuf_tensor("rows", [8, 2048], F32))
        x2r = E(nc.sbuf_tensor("x2r", [8, 2048], F32))
        rsq = E(nc.sbuf_tensor("rsq", [8, 2048], F32))
        stats = E(nc.sbuf_tensor("stats", [8, 16], F32))
        psA = E(nc.psum_tensor("psA", [8, 2048], F32))

        with nc.Block() as blk:
            e_ld = S.op("gpsimd", lambda g: g.dma_start(out=pf[:],
                                                        in_=ps_in), inc="lc")
            e_sl = S.op("gpsimd", lambda g: g.dma_start(out=selb[:],
                                                        in_=sel_in), inc="lc",
                        after=[e_ld])
            e_x2r = S.op("gpsimd", lambda g: g.dma_start(
                out=x2r[:], in_=x2r_in[0]), inc="lxa0")
            e_x2rl = S.op("gpsimd", lambda g: g.dma_start(
                out=rsq[:], in_=x2r_in[1]), inc="lxa0", after=[e_x2r])
            e_x2a = S.op("vector", lambda v: v.tensor_tensor(
                x2r[:], x2r[:], rsq[:], ALU.add), inc="ve",
                after=[e_x2r, e_x2rl])
            e_x2r = e_x2a
            e_hi = S.op("vector", lambda v: v.tensor_copy(pb[:], pf[:]),
                        inc="ve", after=[e_ld])
            e_r0 = S.op("vector", lambda v: v.tensor_copy(rf[:], pb[:]),
                        inc="ve", after=[e_hi])
            e_r1 = S.op("vector", lambda v: v.tensor_tensor(
                rf[:], pf[:], rf[:], ALU.subtract), inc="ve", after=[e_r0])
            e_rb = S.op("vector", lambda v: v.tensor_copy(rb[:], rf[:]),
                        inc="ve", after=[e_r1])
            mm_last = None
            for dcn in range(4):
                def mma(t, dcn=dcn):
                    return t.matmul(
                        psA[:, dcn * 512:(dcn + 1) * 512], selb[:],
                        pb[:, dcn * 512:(dcn + 1) * 512],
                        start=True, stop=False)
                def mmb(t, dcn=dcn):
                    return t.matmul(
                        psA[:, dcn * 512:(dcn + 1) * 512], selb[:],
                        rb[:, dcn * 512:(dcn + 1) * 512],
                        start=False, stop=True)
                S.op("tensor", mma, inc="te", after=[e_hi, e_sl])
                mm_last = S.op("tensor", mmb, inc="te", after=[e_rb])
            e_ev = S.op("vector", lambda v: v.tensor_copy(rows[:], psA[:]),
                        inc="ve", after=[mm_last])
            e_sum = S.op("vector", lambda v: v.tensor_reduce(
                stats[:, 0:1], rows[:], AX, ALU.add), inc="ve", after=[e_ev])
            e_sq = S.op("scalar", lambda s: s.activation(
                rsq[:], rows[:], AF.Square, accum_out=stats[:, 1:2]),
                inc="se", after=[e_ev, e_x2a])

            c1 = S.op("vector", lambda v: v.tensor_scalar(
                stats[:, 2:3], stats[:, 0:1], 1.0 / D, None, ALU.mult),
                inc="ve", after=[e_sum])
            c2 = S.op("vector", lambda v: v.tensor_tensor(
                stats[:, 4:5], stats[:, 2:3], stats[:, 2:3], ALU.mult),
                inc="ve", after=[c1])
            c3 = S.op("vector", lambda v: v.tensor_scalar(
                stats[:, 3:4], stats[:, 1:2], 1.0 / D, None, ALU.mult),
                inc="ve", after=[e_sq])
            c4 = S.op("vector", lambda v: v.tensor_tensor(
                stats[:, 3:4], stats[:, 3:4], stats[:, 4:5], ALU.subtract),
                inc="ve", after=[c2, c3])
            c5 = S.op("vector", lambda v: v.tensor_scalar(
                stats[:, 3:4], stats[:, 3:4], EPS, None, ALU.add),
                inc="ve", after=[c4])
            e_ch = S.op("vector", lambda v: v.reciprocal(
                stats[:, 5:6], stats[:, 3:4]), inc="ve", after=[c5])
            e_rstd = S.op("scalar", lambda s: s.activation(
                stats[:, 6:7], stats[:, 5:6], AF.Sqrt), inc="se", after=[e_ch])
            e_ap = S.op("vector", lambda v: v.tensor_scalar(
                rows[:], rows[:], stats[:, 2:3], stats[:, 6:7],
                ALU.subtract, ALU.mult), inc="ve", after=[e_rstd])
            e_fin = S.op("vector", lambda v: v.tensor_tensor(
                rows[:], rows[:], x2r[:], ALU.add), inc="ve",
                after=[e_ap, e_x2r])
            S.op("sync", lambda sy: sy.dma_start(out=rows_out, in_=rows[:]),
                 inc="st", after=[e_fin])
            stt = S.counts.get("st", 0)
            S.op("sync", lambda sy: sy.wait_ge(sems["st"], stt))
            S.emit(blk, sems)
    return nc


def _host_prep(inputs):
    f32 = np.float32
    x = np.ascontiguousarray(inputs["L0_attn_hidden_states"], dtype=f32)
    Wq = np.ascontiguousarray(inputs["L0_attn_Wq"], dtype=f32)
    Wk = np.ascontiguousarray(inputs["L0_attn_Wk"], dtype=f32)
    Wv = np.ascontiguousarray(inputs["L0_attn_Wv"], dtype=f32)
    Wo = np.ascontiguousarray(inputs["L0_attn_Wo"], dtype=f32)
    bo = np.ascontiguousarray(inputs["L0_attn_bo"], dtype=f32)
    w1 = np.ascontiguousarray(inputs["L0_ffn_w1"], dtype=f32)
    w3 = np.ascontiguousarray(inputs["L0_ffn_w3"], dtype=f32)
    w2 = np.ascontiguousarray(inputs["L0_ffn_w2"], dtype=f32)
    wr = np.ascontiguousarray(inputs["L0_ffn_wr"], dtype=f32)

    shared = {}
    shared["wq_img"] = np.ascontiguousarray(
        Wq.reshape(16, 128, 16, 128).transpose(0, 3, 2, 1)).reshape(16, 128, 2048)
    shared["wk_img"] = np.ascontiguousarray(
        Wk.reshape(16, 128, 16, 128).transpose(0, 3, 2, 1)).reshape(16, 128, 2048)
    shared["wvl_img"] = np.ascontiguousarray(
        Wv[0:1920].reshape(5, 384, 16, 128).transpose(0, 3, 2, 1))
    shared["wvg_img"] = np.ascontiguousarray(
        Wv[1920:2048].reshape(128, 16, 128).transpose(2, 1, 0))
    shared["wo_img"] = np.ascontiguousarray(
        Wo.reshape(4, 512, 16, 128).transpose(0, 3, 2, 1))
    shared["bo_in"] = bo.reshape(1, 2048)
    shared["w1_img"] = np.ascontiguousarray(
        w1.reshape(16, 128, 16, 128).transpose(0, 3, 2, 1)).reshape(16, 128, 2048)
    shared["w3_img"] = np.ascontiguousarray(
        w3.T.reshape(16, 128, 4, 512).transpose(2, 1, 0, 3))
    shared["wr_img"] = np.ascontiguousarray(
        wr.reshape(8, 16, 128).transpose(2, 1, 0))
    shared["tri_in"] = np.ascontiguousarray(
        np.triu(np.ones((128, 128), dtype=f32)))
    shared["idn_in"] = np.eye(128, dtype=f32)
    sel2 = np.zeros((2, 128), dtype=f32)
    sel2[0, 0:64] = 1.0
    sel2[1, 64:128] = 1.0
    shared["sel2_in"] = sel2
    psw = np.zeros((128, 128), dtype=f32)
    for m in range(128):
        psw[m ^ 32, m] = 1.0
    shared["pswap_in"] = psw

    import jax
    import jax.numpy as jnp
    dpat = np.arange(128) % 64
    ipat = dpat % 32
    sgn = np.where(dpat < 32, np.float32(-1.0), np.float32(1.0)).astype(f32)
    with jax.default_device(jax.devices("cpu")[0]):
        _dim = 64
        _invf = 1.0 / 10000.0 ** (jnp.arange(0, _dim, 2, dtype=jnp.float32) / _dim)
        _tabs = {}
        for _h in (0, 1):
            _qe = 1024 * (_h + 1)
            _pos = jnp.arange(_qe - 2048, _qe)
            _fr = _pos[:, None].astype(jnp.float32) * _invf[None, :]
            _tabs[_h] = (np.asarray(jnp.cos(_fr)), np.asarray(jnp.sin(_fr)))

    per_core = []
    for c in range(8):
        b, h = c >> 1, c & 1
        q_end = 1024 * (h + 1)
        _cos32, _sin32 = _tabs[h]
        cosT = _cos32[:, ipat].T.astype(f32)
        sinT = (sgn[None, :] * _sin32[:, ipat]).T.astype(f32)
        if h == 0:
            x_c = np.concatenate(
                [np.zeros((1024, D), dtype=f32), x[b, 0:1024]], axis=0)
        else:
            x_c = x[b]
        bias = np.full((128, 56), -CSTAB, dtype=f32)
        bs_loc = 0 if h == 0 else 512
        for qc in range(2):
            for kt in range(LOC_NKT[qc]):
                ka = (np.arange(128) + kt * 128) + (q_end - 1536)
                m = ka < max(0, bs_loc)
                bias[m, qc * 12 + kt] = NEGBIG
            for kt in range(GLB_NKT[qc]):
                ka = (np.arange(128) + kt * 128) + (q_end - 2048)
                m = ka < 0
                bias[m, 24 + qc * 16 + kt] = NEGBIG
        per_core.append(dict(x_in=np.ascontiguousarray(x_c),
                             cos_in=np.ascontiguousarray(cosT),
                             sin_in=np.ascontiguousarray(sinT),
                             biasv_in=np.ascontiguousarray(bias)))

    w2_img = np.ascontiguousarray(w2.T.reshape(16, 128, 2048))
    shared["w2_img"] = w2_img
    return shared, per_core, w2_img


def kernel(**inputs):
    shared, per_core, w2_img = _host_prep(inputs)
    nc, _ = build_main()
    in_maps = []
    for c in range(8):
        m = dict(shared)
        m.update(per_core[c])
        in_maps.append(m)
    res = run_bass_kernel_spmd(nc, in_maps, core_ids=list(range(8)))

    outs = [res.results[c] for c in range(8)]
    pss = np.stack([o["out_ps"] for o in outs], axis=0)
    x2rows = np.ascontiguousarray(outs[0]["out_x2"][0:8, :])

    nc2 = build_fix()
    ps64 = pss.astype(np.float64)
    ps_hi = ps64.astype(np.float32)
    ps_lo = (ps64 - ps_hi).astype(np.float32)
    ps_in = np.ascontiguousarray(
        np.concatenate([ps_hi, ps_lo], axis=0).reshape(128, 2048))
    selm = np.zeros((128, 8), dtype=np.float32)
    selm[np.arange(128), np.arange(128) % 8] = 1.0
    x264 = x2rows.astype(np.float64)
    x2_hi = x264.astype(np.float32)
    x2_lo = (x264 - x2_hi).astype(np.float32)
    res2 = run_bass_kernel_spmd(
        nc2, [dict(ps_in=ps_in, sel_in=selm,
                   x2r_in=np.stack([x2_hi, x2_lo], axis=0))], core_ids=[0])
    rows = res2.results[0]["rows_out"]

    out = np.empty((B, L, D), dtype=np.float32)
    for c in range(8):
        b, h = c >> 1, c & 1
        out[b, h * 1024:(h + 1) * 1024, :] = outs[c]["out_x2"]
    out[0, 0:8, :] = rows
    return out
